# version 07
# baseline (speedup 1.0000x reference)
import numpy as np
import ml_dtypes
import concourse.bass as bass
import concourse.mybir as mybir
from concourse.bass_utils import run_bass_kernel_spmd
from concourse.replica_groups import maybe_share_collective_output_space

F32 = mybir.dt.float32
BF16 = mybir.dt.bfloat16
AF = mybir.ActivationFunctionType
OP = mybir.AluOpType
AX = mybir.AxisListType.X
MAGIC = float(np.float32(1.5 * 2**23))

NCORES = 8
D = 2048
H = 8192
TPC = 1024
NT = 8
NCH = 16
RS = 256
FCS = 1024
SHRINKS = [float(np.float32(s)) for s in np.linspace(0.4, 1.0, 16).astype(np.float32)]
NSH = 16
QB = 64
NB = 32

_DBG = ()
_DBG_SHAPES = {
    "wsc": ([128, 16], F32), "kb0": ([128, 2048], BF16), "cb0": ([128, 64], F32),
    "fch0": ([128, TPC], F32), "iech0": ([128, TPC], BF16), "seg": ([128, 32], F32),
    "hin": ([128, 16], F32), "hch0": ([128, TPC], F32), "gss": ([1, TPC], F32),
    "opre0": ([128, TPC], F32), "y1t0": ([128, D], F32), "xln0": ([128, D], F32),
    "xlnt1": ([128, D], F32), "v2t1": ([128, D], F32), "hh31": ([128, TPC], BF16),
    "ibg0": ([128, 16 * TPC], BF16), "ibg1": ([128, 16 * TPC], BF16),
    "ibg2": ([128, 16 * TPC], BF16), "ibg3": ([128, 16 * TPC], BF16),
    "hh0": ([128, TPC], BF16), "u0": ([128, D], F32), "err0": ([128, NSH * NB], F32),
    "ko0": ([128, D], BF16), "stw": ([128, 4], F32), "row8": ([1, 8], F32),
}


class Ctr:
    def __init__(self, nc, name):
        self.h = nc.semaphore(name).__enter__()
        self.n = 0

    def sig(self, instr, k=1):
        instr.then_inc(self.h, k)
        self.n += k
        return self.n

    def dma(self, instr):
        return self.sig(instr, 16)

    def wait(self, eng, val=None):
        v = self.n if val is None else val
        assert v <= self.n
        if v > 0:
            eng.wait_ge(self.h, v)


def r3(ap, k):
    return ap.rearrange("p (b k) -> p b k", k=k)


def build_program(dbg_names=()):
    nc = bass.Bass(target_bir_lowering=False, detect_race_conditions=False)

    x_d = nc.dram_tensor("x", [TPC, D], F32, kind="ExternalInput").ap()
    w_d = {p: nc.dram_tensor("w" + p, [RS, D], F32, kind="ExternalInput").ap()
           for p in ("i", "f", "g", "o")}
    wfc_d = nc.dram_tensor("wfc", [FCS, D], F32, kind="ExternalInput").ap()
    wpj_d = nc.dram_tensor("wpj", [RS, H], F32, kind="ExternalInput").ap()
    nrm_d = {p: nc.dram_tensor("nrm" + p, [128, D], F32, kind="ExternalInput").ap()
             for p in ("i", "f", "g", "o")}
    gwc_d = nc.dram_tensor("gwcols", [128, NCH], F32, kind="ExternalInput").ap()
    smask_d = nc.dram_tensor("smask", [128, 16], F32, kind="ExternalInput").ap()
    y_d = nc.dram_tensor("y", [TPC, D], F32, kind="ExternalOutput").ap()

    ident_f = nc.inline_tensor(np.eye(128, dtype=np.float32), name="ident_f").ap()
    ident_b = nc.inline_tensor(np.eye(128, dtype=ml_dtypes.bfloat16), name="ident_b").ap()
    ones_f = nc.inline_tensor(np.ones((128, 128), np.float32), name="ones_f").ap()
    bias_np = np.zeros((128, 2), np.float32); bias_np[:, 0] = 1e-8; bias_np[:, 1] = 1e-5
    bias_d = nc.inline_tensor(bias_np, name="bias_c").ap()

    RG = [list(range(NCORES))]
    sp_ar = maybe_share_collective_output_space("AllReduce", RG)
    sp_ag = maybe_share_collective_output_space("AllGather", RG)
    ag0_in = nc.dram_tensor("ag0_in", [1, 8], F32, kind="Internal").ap()
    ag0_out = nc.dram_tensor("ag0_out", [1, 8], F32, kind="Internal", addr_space=sp_ar).ap()
    ag1_in = nc.dram_tensor("ag1_in", [D, 4 * RS], BF16, kind="Internal").ap()
    ag1_out = nc.dram_tensor("ag1_out", [NCORES * D, 4 * RS], BF16, kind="Internal", addr_space=sp_ag).ap()
    ag2a_in = nc.dram_tensor("ag2a_in", [D, FCS], BF16, kind="Internal").ap()
    ag2a_out = nc.dram_tensor("ag2a_out", [NCORES * D, FCS], BF16, kind="Internal", addr_space=sp_ag).ap()
    ag2b_in = nc.dram_tensor("ag2b_in", [H, RS], BF16, kind="Internal").ap()
    ag2b_out = nc.dram_tensor("ag2b_out", [NCORES * H, RS], BF16, kind="Internal", addr_space=sp_ag).ap()
    ag3_in = nc.dram_tensor("ag3_in", [128, 32], F32, kind="Internal").ap()
    ag3_out = nc.dram_tensor("ag3_out", [NCORES * 128, 32], F32, kind="Internal", addr_space=sp_ag).ap()

    dbg = {nm: nc.dram_tensor("dbg_" + nm, _DBG_SHAPES[nm][0], _DBG_SHAPES[nm][1],
                              kind="ExternalOutput").ap() for nm in dbg_names}

    def S(name, shape, dt):
        return nc.sbuf_tensor(name, shape, dt).__enter__()

    kbuf = S("kbuf", [128, NCH * TPC], BF16)
    fbuf = S("fbuf", [128, NCH * TPC], F32)
    ibuf = S("ibuf", [128, NCH * TPC], BF16)
    scr = [S(f"scr{i}", [128, D], F32) for i in range(4)]
    scb = [S("scb0", [128, D], BF16)]
    cst = S("cst", [128, D], F32)
    wslab = S("wslab", [128, 8192], BF16)
    wsv = wslab[:, 0:2048]
    wsv2 = wslab[:, 2048:4096]
    zeros = S("zeros_t", [128, TPC], F32)
    idf = S("idf", [128, 128], F32)
    idb = S("idb", [128, 128], BF16)
    onesf = S("onesf", [128, 128], F32)
    gwcols = S("gwcols_s", [128, NCH], F32)
    smask = S("smask_s", [128, 16], F32)
    stats = S("stats", [128, 128], F32)
    errst = S("errst", [128, NSH * NB], F32)
    eqb = S("eqb", [128, 256], F32)
    eqc = S("eqc", [128, NSH * NB], F32)
    cbs = S("cbs", [128, 64], F32)
    gath = S("gath", [128, 256], F32)
    hin = S("hin", [128, NCH], F32)
    junk = S("junk_t", [128, 8], F32)
    wsc = S("wsc", [128, 16], F32)
    row8 = S("row8", [1, 8], F32)
    biasc = S("biasc", [128, 2], F32)
    gssr = S("gssr", [1, TPC], F32)

    ST_SS = 0
    ST_RSTD = 8
    ST_GRS = 24
    ST_MU = 48
    ST_RSN = 56
    ST_W = 64
    ST_T0 = 68
    ST_MU2 = 80
    ST_RSN2 = 88

    psA = nc.psum_tensor("psA", [128, TPC], F32).__enter__()
    psB = nc.psum_tensor("psB", [128, TPC], F32).__enter__()
    psTb = nc.psum_tensor("psTb", [128, 1024], BF16).__enter__()
    psTf = nc.psum_tensor("psTf", [128, 512], F32).__enter__()
    psS = nc.psum_tensor("psS", [128, 512], F32).__enter__()

    clx = Ctr(nc, "clx")
    clw = Ctr(nc, "clw")
    cls = Ctr(nc, "cls")
    cs0 = Ctr(nc, "cs0")
    cs1 = Ctr(nc, "cs1")
    cs2 = Ctr(nc, "cs2")
    cs3 = Ctr(nc, "cs3")
    cso = Ctr(nc, "cso")
    cg = Ctr(nc, "cg")
    cc = Ctr(nc, "cc")
    cp = Ctr(nc, "cp")
    cv = Ctr(nc, "cv")
    ca = Ctr(nc, "ca")
    cq = Ctr(nc, "cq")
    cpl = Ctr(nc, "cpl")

    block = nc.Block().__enter__()
    ENG = {"sync": "sync", "dve": "vector", "act": "scalar", "pe": "tensor", "pool": "gpsimd"}

    DATAPATH = {"tensor_copy", "tensor_tensor", "tensor_scalar", "tensor_reduce",
                "scalar_tensor_tensor", "tensor_tensor_scan", "reciprocal", "memset",
                "memzero", "activation", "copy", "mul", "add", "square", "sqrt",
                "tensor_add", "tensor_sub", "tensor_mul", "tensor_max", "select",
                "tensor_single_scalar", "tensor_relu", "reduce_max", "reduce_sum"}

    class DrainWrap:
        def __init__(self, eng):
            self._e = eng

        def __getattr__(self, n):
            a = getattr(self._e, n)
            if n in DATAPATH:
                def call(*args, _a=a, **kw):
                    ins = _a(*args, **kw)
                    need = True
                    try:
                        out = kw.get("out")
                        if out is None and args:
                            out = args[0]
                        sh = out.shape
                        free = 1
                        for d in sh[1:]:
                            free *= int(d)
                        need = free < 512 or kw.get("accum_out") is not None
                    except Exception:
                        need = True
                    if need:
                        return self._e.drain()
                    return ins
                return call
            return a

    def em(engine, fn, wrap=True):
        if wrap and engine in ("dve", "act", "pool"):
            getattr(block, ENG[engine])(lambda e: fn(DrainWrap(e)))
        else:
            getattr(block, ENG[engine])(fn)

    def vsp(v):
        v.tensor_copy(out=junk[:, 0:1], in_=zeros[:, 0:1])

    def ssp(s):
        pass

    def TS(v, out, in0, s1, s2=None, op0=OP.mult, op1=None):
        if op1 is None:
            return v.tensor_scalar(out=out, in0=in0, scalar1=s1, scalar2=None, op0=op0)
        return v.tensor_scalar(out=out, in0=in0, scalar1=s1, scalar2=s2, op0=op0, op1=op1)

    def col(t, c):
        return t[:, c:c + 1]

    def dbg_tap(name, src_ap, gate=None):
        if name in dbg:
            def f(sy, name=name, src=src_ap, gate=gate):
                if gate is not None:
                    gate.wait(sy)
                cso.dma(sy.dma_start(out=dbg[name][:], in_=src))
            em("sync", f)
            for e in ("dve", "act", "pe", "pool"):
                def g(eng):
                    cso.wait(eng)
                em(e, g)

    def emit_transpose16_bf(src_tile, dst_fn):
        for half in range(2):
            def f(t, half=half, src=src_tile):
                ca.wait(t)
                for j in range(8):
                    c = 8 * half + j
                    ins = t.transpose(out=psTb[:, 128 * j:128 * (j + 1)],
                                      in_=src[:, 128 * c:128 * (c + 1)], identity=idb[:])
                cp.sig(ins)
            em("pe", f)

            def f(s, half=half, dst_fn=dst_fn):
                cp.wait(s)
                for j in range(8):
                    c = 8 * half + j
                    ins = s.copy(out=dst_fn(c), in_=psTb[:, 128 * j:128 * (j + 1)])
                ca.sig(ins)
            em("act", f)

    def f(sy):
        clw.dma(sy.dma_start(out=idf[:], in_=ident_f[:]))
        clw.dma(sy.dma_start(out=idb[:], in_=ident_b[:]))
        clw.dma(sy.dma_start(out=onesf[:], in_=ones_f[:]))
        clw.dma(sy.dma_start(out=gwcols[:], in_=gwc_d[:]))
        clw.dma(sy.dma_start(out=smask[:], in_=smask_d[:]))
        clw.dma(sy.dma_start(out=biasc[:], in_=bias_d[:]))
    em("sync", f)

    def f(v):
        clw.wait(v)
        v.memset(zeros[:], 0.0)
        v.memset(junk[:], 0.0)
        v.memset(stats[:], 0.0)
        v.memset(eqb[:], 0.0)
        v.memset(errst[:], 0.0)
        v.memset(eqc[:], 0.0)
        v.memset(cbs[:], 0.0)
        v.memset(hin[:], 0.0)
        v.memset(gath[:], 0.0)
        v.memset(wsc[:], 0.0)
        cv.sig(v.memset(gssr[:], 0.0))
    em("dve", f)
    ev_consts = cv.n

    for wi, p in enumerate("ifgo"):
        def f(sy, p=p):
            cv.wait(sy)
            clw.dma(sy.dma_start(out=scr[0][:], in_=w_d[p][0:128, :]))
            clw.dma(sy.dma_start(out=scr[1][:], in_=w_d[p][128:256, :]))
        em("sync", f)

        def f(v, wi=wi):
            clw.wait(v)
            cv.wait(v, ev_consts)
            v.tensor_reduce(out=col(stats, ST_T0), in_=scr[0][:], axis=AX, op=OP.add,
                            apply_absolute_value=True)
            v.tensor_reduce(out=col(stats, ST_T0 + 1), in_=scr[1][:], axis=AX, op=OP.add,
                            apply_absolute_value=True)
            vsp(v)
            cv.sig(v.tensor_tensor(out=col(stats, ST_W + wi), in0=col(stats, ST_T0),
                                   in1=col(stats, ST_T0 + 1), op=OP.add))
        em("dve", f)
    ev_wparts = cv.n
    dbg_tap("stw", stats[:, ST_W:ST_W + 4], gate=cv)

    def f(t):
        cv.wait(t, ev_wparts)
        cp.sig(t.matmul(out=psS[0:1, 0:4], lhsT=onesf[:, 0:1],
                        rhs=stats[:, ST_W:ST_W + 4], start=True, stop=True))
    em("pe", f)
    ev_pp = cp.n

    def f(s):
        cp.wait(s, ev_pp)
        s.memzero(row8[:])
        ca.sig(s.copy(out=row8[0:1, 0:4], in_=psS[0:1, 0:4]))
    em("act", f)
    ev_row8 = ca.n

    def f(sy):
        ca.wait(sy, ev_row8)
        cs0.dma(sy.dma_start(out=ag0_in[:], in_=row8[:]))
    em("sync", f)
    ev_ag0st = cs0.n

    def f(g):
        cs0.wait(g, ev_ag0st)
        cc.sig(g.collective_compute("AllReduce", OP.add, replica_groups=RG,
                                    ins=[ag0_in[:].opt()], outs=[ag0_out[:].opt()]))
        cc.wait(g)
        cg.dma(g.dma_start(out=row8[:], in_=ag0_out[:]))
    em("pool", f)
    ev_ag0 = cg.n

    def f(v):
        cg.wait(v, ev_ag0)
        TS(v, row8[0:1, 4:8], row8[0:1, 0:4], 1.0 / float(D * D), 1e-5, OP.mult, OP.max)
        vsp(v)
        cv.sig(v.reciprocal(out=row8[0:1, 0:4], in_=row8[0:1, 4:8]))
    em("dve", f)
    ev_wsc1 = cv.n
    dbg_tap("row8", row8[:], gate=cv)

    def f(t):
        cv.wait(t, ev_wsc1)
        cp.sig(t.matmul(out=psS[:, 0:8], lhsT=onesf[0:1, :], rhs=row8[0:1, 0:8],
                        start=True, stop=True))
    em("pe", f)
    ev_wsc2 = cp.n

    def f(s):
        cp.wait(s, ev_wsc2)
        ca.sig(s.copy(out=wsc[:, 0:8], in_=psS[:, 0:8]))
    em("act", f)
    ev_wsc = ca.n
    dbg_tap("wsc", wsc[:], gate=ca)

    for wi, p in enumerate("ifgo"):
        for rt in range(2):
            def f(sy, p=p, rt=rt):
                cv.wait(sy)
                ca.wait(sy)
                cs1.wait(sy)
                clw.dma(sy.dma_start(out=scr[0][:], in_=w_d[p][128 * rt:128 * (rt + 1), :]))
            em("sync", f)

            def f(v, wi=wi):
                clw.wait(v)
                ca.wait(v, ev_wsc)
                TS(v, scr[1][:], scr[0][:], col(wsc, wi))
                TS(v, scr[2][:], scr[1][:], MAGIC, MAGIC, OP.add, OP.subtract)
                cv.sig(TS(v, scb[0][:], scr[2][:], 1.0, -1.0, OP.min, OP.max))
            em("dve", f)

            def f(t):
                cv.wait(t)
            em("pe", f)
            emit_transpose16_bf(scb[0][:], lambda c: wsv[:, 128 * c:128 * (c + 1)])

            def f(sy, wi=wi, rt=rt):
                ca.wait(sy)
                c0 = 256 * wi + 128 * rt
                cs1.dma(sy.dma_start(
                    out=ag1_in[:, c0:c0 + 128].rearrange("(c p) j -> p c j", p=128),
                    in_=r3(wsv, 128)))
            em("sync", f)
    ev_t_staged = cs1.n

    def f(g):
        cs1.wait(g, ev_t_staged)
        cc.sig(g.collective_compute("AllGather", OP.bypass, replica_groups=RG,
                                    ins=[ag1_in[:].opt()], outs=[ag1_out[:].opt()]))
    em("pool", f)
    ev_ag1 = cc.n

    ALPH = [float(a) for a in np.float32(np.linspace(0.4, 1.0, 16))]
    C15 = float(np.float32(np.float32(1.0) / np.float32(15.0)))
    for ti in range(16):
        if ti < 8:
            src = wfc_d[128 * ti:128 * (ti + 1), :]
        else:
            rt, csl = (ti - 8) // 4, (ti - 8) % 4
            src = wpj_d[128 * rt:128 * (rt + 1), 2048 * csl:2048 * (csl + 1)]

        def f(sy, src=src):
            cv.wait(sy)
            ca.wait(sy)
            cs1.wait(sy)
            cs2.wait(sy)
            clw.dma(sy.dma_start(out=scr[0][:], in_=src))
        em("sync", f)

        def f(v):
            clw.wait(v)
            v.tensor_reduce(out=eqb[:, 0:32], in_=r3(scr[0][:], QB), axis=AX,
                            op=OP.max, apply_absolute_value=True)
            vsp(v)
            cv.sig(TS(v, eqb[:, 32:64], eqb[:, 0:32], 1e-8, C15,
                      OP.max, OP.mult))
        em("dve", f)

        dT = (fbuf[:, 0:2048], fbuf[:, 2048:4096])
        qT4 = (fbuf[:, 4096:6144], fbuf[:, 6144:8192],
               fbuf[:, 8192:10240], fbuf[:, 10240:12288])
        pT = fbuf[:, 12288:14336]
        qB3 = (scr[2][:], scr[3][:], fbuf[:, 14336:16384])
        SKC5 = (eqc[:, 0:32], eqc[:, 192:224], eqc[:, 256:288],
                eqc[:, 384:416], eqc[:, 416:448])
        RKC2 = (eqc[:, 32:64], eqc[:, 224:256])
        BERR = eqc[:, 64:96]
        BSC = eqc[:, 96:128]
        DTMP2 = (eqc[:, 128:160], eqc[:, 352:384])
        DIF = (eqc[:, 160:192], eqc[:, 320:352])

        for k in range(NSH):
            def f(v, k=k):
                j = k - 3
                ej = errst[:, 32 * j:32 * (j + 1)]
                if k == 0:
                    TS(v, SKC5[0], eqb[:, 32:64], ALPH[0])
                    v.drain()
                    v.reciprocal(out=RKC2[0], in_=SKC5[0])
                    v.drain()
                rke = r3(RKC2[k % 2], 1).broadcast_to([128, NB, QB])
                v.tensor_tensor(out=r3(scr[1][:], QB), in0=r3(scr[0][:], QB),
                                in1=rke, op=OP.mult)
                if k < NSH - 1:
                    TS(v, SKC5[(k + 1) % 5], eqb[:, 32:64], ALPH[k + 1])
                if k >= 3:
                    ca.wait(v)
                    v.tensor_reduce(out=ej, in_=r3(qT4[j % 4], QB),
                                    axis=AX, op=OP.add)
                if k >= 5:
                    v.tensor_tensor(out=DIF[(k - 4) % 2], in0=DIF[(k - 4) % 2],
                                    in1=DTMP2[(k - 4) % 2], op=OP.mult)
                TS(v, qB3[k % 3], scr[1][:], MAGIC, MAGIC, OP.add, OP.subtract)
                if k < NSH - 1:
                    v.reciprocal(out=RKC2[(k + 1) % 2], in_=SKC5[(k + 1) % 5])
                if k >= 5:
                    v.tensor_tensor(out=BSC, in0=BSC, in1=DIF[(k - 4) % 2],
                                    op=OP.add)
                cq.sig(TS(v, qB3[k % 3], qB3[k % 3], 15.0, -15.0,
                          OP.min, OP.max))
                if k == 3:
                    v.tensor_copy(out=BERR, in_=errst[:, 0:32])
                if k >= 4:
                    v.tensor_tensor(out=DTMP2[j % 2], in0=ej, in1=BERR,
                                    op=OP.subtract)
                v.drain()
                if k == 3:
                    v.tensor_copy(out=BSC, in_=SKC5[0])
                if k >= 4:
                    TS(v, DTMP2[j % 2], DTMP2[j % 2], 0.0, None, OP.is_lt)
                    v.tensor_tensor(out=DIF[j % 2], in0=SKC5[j % 5],
                                    in1=BSC, op=OP.subtract)
                    v.tensor_tensor(out=BERR, in0=BERR, in1=ej, op=OP.min)
            em("dve", f, wrap=False)

            def f(g, k=k):
                cq.wait(g)
                ske = r3(SKC5[k % 5], 1).broadcast_to([128, NB, QB])
                g.tensor_tensor(out=r3(pT, QB), in0=r3(qB3[k % 3], QB),
                                in1=ske, op=OP.mult)
                cpl.sig(g.tensor_tensor(out=dT[k % 2], in0=pT, in1=scr[0][:],
                                        op=OP.subtract))
            em("pool", f, wrap=False)

            def f(s, k=k):
                cpl.wait(s)
                ca.sig(s.activation(out=qT4[k % 4], in_=dT[k % 2], func=AF.Square))
            em("act", f)

        def f(v):
            def wj():
                v.tensor_copy(out=scr[1][:, 0:512], in_=scr[0][:, 0:512])
            def redsel(jj, first):
                e = errst[:, 32 * jj:32 * (jj + 1)]
                ca.wait(v, ca.n - (15 - jj))
                v.tensor_reduce(out=e, in_=r3(qT4[jj % 4], QB), axis=AX, op=OP.add)
                wj()
                if not first:
                    v.tensor_tensor(out=DIF[(jj - 1) % 2], in0=DIF[(jj - 1) % 2],
                                    in1=DTMP2[(jj - 1) % 2], op=OP.mult)
                    wj()
                    v.tensor_tensor(out=BSC, in0=BSC, in1=DIF[(jj - 1) % 2],
                                    op=OP.add)
                    wj()
                else:
                    v.tensor_tensor(out=DIF[0], in0=DIF[0], in1=DTMP2[0],
                                    op=OP.mult)
                    wj()
                    v.tensor_tensor(out=BSC, in0=BSC, in1=DIF[0], op=OP.add)
                    wj()
                v.tensor_tensor(out=DTMP2[jj % 2], in0=e, in1=BERR, op=OP.subtract)
                v.drain()
                TS(v, DTMP2[jj % 2], DTMP2[jj % 2], 0.0, None, OP.is_lt)
                wj()
                v.tensor_tensor(out=DIF[jj % 2], in0=SKC5[jj % 5], in1=BSC,
                                op=OP.subtract)
                v.drain()
                if jj < 15:
                    v.tensor_tensor(out=BERR, in0=BERR, in1=e, op=OP.min)
                    v.drain()
            redsel(13, True)
            redsel(14, False)
            redsel(15, False)
            wj()
            v.tensor_tensor(out=DIF[1], in0=DIF[1], in1=DTMP2[1], op=OP.mult)
            wj()
            v.tensor_tensor(out=BSC, in0=BSC, in1=DIF[1], op=OP.add)
            cv.sig(v.drain())
        em("dve", f, wrap=False)
        if ti == 0:
            dbg_tap("err0", errst[:], gate=cv)

        def f(v):
            sb = eqc[:, 96:128]
            vsp(v)
            v.reciprocal(out=eqc[:, 0:32], in_=sb)
            vsp(v)
            v.tensor_copy(out=cbs[:, 32:64], in_=sb)
            rse = r3(eqc[:, 0:32], 1).broadcast_to([128, NB, QB])
            sbe = r3(sb, 1).broadcast_to([128, NB, QB])
            v.tensor_tensor(out=r3(scr[1][:], QB), in0=r3(scr[0][:], QB),
                            in1=rse, op=OP.mult)
            TS(v, scr[2][:], scr[1][:], MAGIC, MAGIC, OP.add, OP.subtract)
            TS(v, scr[2][:], scr[2][:], 15.0, -15.0, OP.min, OP.max)
            v.tensor_tensor(out=r3(scr[3][:], QB), in0=r3(scr[2][:], QB),
                            in1=sbe, op=OP.mult)
            cv.sig(v.tensor_copy(out=wsv2, in_=scr[3][:]))
        em("dve", f)
        if ti == 0:
            dbg_tap("cb0", cbs[:], gate=cv)

        def f(t):
            cv.wait(t)
        em("pe", f)
        emit_transpose16_bf(wsv2, lambda c: wsv[:, 128 * c:128 * (c + 1)])

        def f(sy, ti=ti):
            ca.wait(sy)
            if ti < 8:
                dst = ag2a_in[:, 128 * ti:128 * (ti + 1)]
            else:
                rt, csl = (ti - 8) // 4, (ti - 8) % 4
                dst = ag2b_in[2048 * csl:2048 * (csl + 1), 128 * rt:128 * (rt + 1)]
            cs2.dma(sy.dma_start(out=dst.rearrange("(c p) j -> p c j", p=128),
                                 in_=r3(wsv, 128)))
        em("sync", f)
    ev_ffn_staged = cs2.n

    def f(g):
        cs2.wait(g, ev_ffn_staged)
        cc.sig(g.collective_compute("AllGather", OP.bypass, replica_groups=RG,
                                    ins=[ag2a_in[:].opt()], outs=[ag2a_out[:].opt()]))
        cc.sig(g.collective_compute("AllGather", OP.bypass, replica_groups=RG,
                                    ins=[ag2b_in[:].opt()], outs=[ag2b_out[:].opt()]))
    em("pool", f)
    ev_ag2 = cc.n

    W_IDX = {"i": 0, "f": 1, "g": 2, "o": 3}

    def emit_stageA(p, with_ss):
        def f(sy, p=p):
            cv.wait(sy)
            ca.wait(sy)
            clw.dma(sy.dma_start(out=cst[:], in_=nrm_d[p][:]))
        em("sync", f)
        for t in range(NT):
            def f(sy, t=t):
                cv.wait(sy)
                ca.wait(sy)
                clx.dma(sy.dma_start(out=scr[0][:], in_=x_d[128 * t:128 * (t + 1), :]))
            em("sync", f)
            if with_ss:
                def f(s, t=t):
                    clx.wait(s)
                    cv.wait(s)
                    s.activation(out=scr[2][:], in_=scr[0][:], func=AF.Square,
                                 accum_out=col(stats, ST_SS + t))
                    ssp(s)
                    s.activation(out=col(stats, ST_T0 + 2), in_=col(stats, ST_SS + t),
                                 func=AF.Ln, scale=1.0 / float(D), bias=col(biasc, 0))
                    ssp(s)
                    ca.sig(s.activation(out=col(stats, ST_RSTD + t), in_=col(stats, ST_T0 + 2),
                                        func=AF.Exp, scale=-0.5))
                em("act", f)

            def f(v, t=t, p=p):
                clx.wait(v)
                clw.wait(v)
                ca.wait(v)
                wi = W_IDX[p]
                v.tensor_tensor(out=scr[1][:], in0=scr[0][:], in1=cst[:], op=OP.mult)
                vsp(v)
                v.tensor_reduce(out=col(stats, ST_T0), in_=scr[1][:], axis=AX,
                                op=OP.max, apply_absolute_value=True)
                vsp(v)
                TS(v, col(stats, ST_T0 + 1), col(stats, ST_T0), col(stats, ST_RSTD + t),
                   1e-5, OP.mult, OP.max)
                vsp(v)
                v.reciprocal(out=col(stats, ST_T0 + 3), in_=col(stats, ST_T0 + 1))
                vsp(v)
                TS(v, col(stats, ST_T0 + 4), col(stats, ST_T0 + 3), col(stats, ST_RSTD + t),
                   127.0, OP.mult, OP.mult)
                TS(v, col(stats, 96 + t), col(stats, ST_T0 + 1), col(wsc, 4 + wi),
                   1.0 / 127.0, OP.mult, OP.mult)
                TS(v, scr[2][:], scr[1][:], col(stats, ST_T0 + 4))
                cv.sig(TS(v, scb[0][:], scr[2][:], MAGIC, MAGIC, OP.add, OP.subtract))
            em("dve", f)

            def f(te):
                cv.wait(te)
            em("pe", f)
            emit_transpose16_bf(scb[0][:],
                                lambda c, t=t: kbuf[:, TPC * c + 128 * t:TPC * c + 128 * (t + 1)])

    def emit_cexp():
        crow = scr[1][0:1, TPC:TPC + TPC]
        for t in range(NT):
            def f(te, t=t):
                cv.wait(te)
                ca.wait(te)
                cp.sig(te.matmul(out=psTf[0:1, 0:128], lhsT=col(stats, 96 + t),
                                 rhs=idf[:], start=True, stop=True))
            em("pe", f)

            def f(sz, t=t, crow=crow):
                cp.wait(sz)
                ca.sig(sz.copy(out=crow[0:1, 128 * t:128 * (t + 1)], in_=psTf[0:1, 0:128]))
            em("act", f)

        def f(te, crow=crow):
            ca.wait(te)
            te.matmul(out=psB[:, 0:512], lhsT=onesf[0:1, :], rhs=crow[0:1, 0:512],
                      start=True, stop=True)
            cp.sig(te.matmul(out=psB[:, 512:1024], lhsT=onesf[0:1, :],
                             rhs=crow[0:1, 512:1024], start=True, stop=True))
        em("pe", f)

        def f(sz):
            cp.wait(sz)
            ca.sig(sz.copy(out=scr[1][:, 0:TPC], in_=psB[:, 0:TPC]))
        em("act", f)

    def emit_fm_matmul(p, evac):
        wi = W_IDX[p]
        emit_cexp()
        for oc in range(NCH):
            if oc % 2 == 0:
                r = oc // 2
                half = r % 2
                def f(sy, r=r, half=half, wi=wi):
                    cp.wait(sy)
                    cs1.wait(sy)
                    cs2.wait(sy)
                    cc.wait(sy, ev_ag1)
                    src = ag1_out[D * r:D * (r + 1), RS * wi:RS * (wi + 1)]
                    cls.dma(sy.dma_start(
                        out=wslab[:, 4096 * half:4096 * half + 4096].rearrange(
                            "p (k j) -> p k j", j=256),
                        in_=src.rearrange("(k p) j -> p k j", p=128)))
                em("sync", f)
            half = (oc // 2) % 2
            ps = psA if oc % 2 == 0 else psB
            def f(t, oc=oc, half=half, ps=ps):
                cls.wait(t)
                cv.wait(t)
                ca.wait(t)
                for cpos in range(2):
                    for k in range(NCH):
                        ins = t.matmul(
                            out=ps[:, 512 * cpos:512 * (cpos + 1)],
                            lhsT=wslab[:, 4096 * half + 256 * k + 128 * (oc % 2):
                                       4096 * half + 256 * k + 128 * (oc % 2) + 128],
                            rhs=kbuf[:, TPC * k + 512 * cpos:TPC * k + 512 * (cpos + 1)],
                            start=(k == 0), stop=(k == NCH - 1))
                cp.sig(ins)
            em("pe", f)
            evac(oc, ps)

    emit_stageA("f", with_ss=True)

    def evac_f(oc, ps):
        def f(v, oc=oc, ps=ps):
            cp.wait(v)
            ca.wait(v)
            cv.sig(v.tensor_tensor(out=scr[2][:, 0:TPC], in0=ps[:, 0:TPC],
                                   in1=scr[1][:, 0:TPC], op=OP.mult))
        em("dve", f)

        def f(s, oc=oc):
            cv.wait(s)
            ca.sig(s.activation(out=fbuf[:, TPC * oc:TPC * (oc + 1)],
                                in_=scr[2][:, 0:TPC], func=AF.Sigmoid))
        em("act", f)
    emit_fm_matmul("f", evac_f)
    dbg_tap("fch0", fbuf[:, 0:TPC], gate=ca)

    emit_stageA("i", with_ss=False)

    def evac_i(oc, ps):
        def f(v, oc=oc, ps=ps):
            cp.wait(v)
            ca.wait(v)
            cv.sig(v.tensor_tensor(out=scr[3][:, 0:TPC], in0=ps[:, 0:TPC],
                                   in1=scr[1][:, 0:TPC], op=OP.mult))
        em("dve", f)

        def f(s, oc=oc):
            cv.wait(s)
            ca.sig(s.activation(out=scr[2][:, 0:TPC], in_=scr[3][:, 0:TPC], func=AF.Silu))
        em("act", f)

        def f(v, oc=oc):
            ca.wait(v)
            TS(v, scr[3][:, 0:TPC], fbuf[:, TPC * oc:TPC * (oc + 1)], -1.0, 1.0,
               OP.mult, OP.add)
            cv.sig(v.tensor_tensor(out=ibuf[:, TPC * oc:TPC * (oc + 1)],
                                   in0=scr[2][:, 0:TPC], in1=scr[3][:, 0:TPC], op=OP.mult))
        em("dve", f)
    emit_fm_matmul("i", evac_i)
    dbg_tap("iech0", ibuf[:, 0:TPC], gate=cv)

    for c in range(NCH):
        def f(v, c=c):
            if c == 0:
                ca.wait(v)
            v.tensor_tensor_scan(out=scr[2][:, 0:TPC], data0=fbuf[:, TPC * c:TPC * (c + 1)],
                                 data1=zeros[:, 0:TPC], initial=1.0, op0=OP.mult, op1=OP.add)
            v.tensor_tensor_scan(out=scr[3][:, 0:TPC], data0=fbuf[:, TPC * c:TPC * (c + 1)],
                                 data1=ibuf[:, TPC * c:TPC * (c + 1)], initial=0.0,
                                 op0=OP.mult, op1=OP.add)
            v.tensor_copy(out=col(cbs, c), in_=scr[2][:, TPC - 1:TPC])
            ins = v.tensor_copy(out=col(cbs, 16 + c), in_=scr[3][:, TPC - 1:TPC])
            if c == NCH - 1:
                cv.sig(ins)
        em("dve", f)
    dbg_tap("seg", cbs[:, 0:32], gate=cv)

    def f(sy):
        cv.wait(sy)
        cs3.dma(sy.dma_start(out=ag3_in[:], in_=cbs[:, 0:32]))
    em("sync", f)
    ev_ag3st = cs3.n

    def f(g):
        cs3.wait(g, ev_ag3st)
        cc.sig(g.collective_compute("AllGather", OP.bypass, replica_groups=RG,
                                    ins=[ag3_in[:].opt()], outs=[ag3_out[:].opt()]))
        cc.wait(g)
        cg.dma(g.dma_start(out=gath[:].rearrange("p (g j) -> p g j", j=32),
                           in_=ag3_out[:].rearrange("(g p) j -> p g j", p=128)))
    em("pool", f)
    ev_gath = cg.n

    def f(v):
        cg.wait(v, ev_gath)
        v.memset(hin[:], 0.0)
        for r in range(NCORES):
            TS(v, scr[2][:, 0:16], gath[:, 32 * r:32 * r + 16], col(smask, r),
               col(smask, 8 + r), OP.mult, OP.add)
            TS(v, scr[2][:, 16:32], gath[:, 32 * r + 16:32 * r + 32], col(smask, r))
            v.tensor_tensor(out=scr[2][:, 32:48], in0=hin[:, 0:16], in1=scr[2][:, 0:16],
                            op=OP.mult)
            v.tensor_tensor(out=hin[:, 0:16], in0=scr[2][:, 32:48], in1=scr[2][:, 16:32],
                            op=OP.add)
        cv.sig(v.tensor_copy(out=junk[:, 3:4], in_=zeros[:, 0:1]))
    em("dve", f)
    dbg_tap("hin", hin[:, 0:16], gate=cv)

    for c in range(NCH):
        def f(v, c=c):
            v.tensor_tensor_scan(out=scr[2][:, 0:TPC], data0=fbuf[:, TPC * c:TPC * (c + 1)],
                                 data1=ibuf[:, TPC * c:TPC * (c + 1)], initial=col(hin, c),
                                 op0=OP.mult, op1=OP.add)
            vsp(v)
            cv.sig(v.tensor_copy(out=fbuf[:, TPC * c:TPC * (c + 1)], in_=scr[2][:, 0:TPC]))
        em("dve", f)

        def f(s, c=c):
            cv.wait(s)
            s.activation(out=scr[3][:, 0:TPC], in_=fbuf[:, TPC * c:TPC * (c + 1)], func=AF.Silu)
            ca.sig(s.copy(out=fbuf[:, TPC * c:TPC * (c + 1)], in_=scr[3][:, 0:TPC]))
        em("act", f)
    ev_silu = ca.n
    dbg_tap("hch0", fbuf[:, 0:TPC], gate=ca)

    emit_stageA("g", with_ss=False)

    def evac_g(oc, ps):
        def f(v, oc=oc, ps=ps):
            cp.wait(v)
            ca.wait(v)
            cv.sig(v.tensor_tensor(out=scr[3][:, 0:TPC], in0=ps[:, 0:TPC],
                                   in1=scr[1][:, 0:TPC], op=OP.mult))
        em("dve", f)

        def f(s, oc=oc):
            cv.wait(s)
            ca.sig(s.activation(out=scr[2][:, 0:TPC], in_=scr[3][:, 0:TPC], func=AF.Square))
        em("act", f)

        def f(t, oc=oc):
            ca.wait(t)
            t.matmul(out=psS[0:1, 0:512], lhsT=onesf[:, 0:1], rhs=scr[2][:, 0:512],
                     start=(oc == 0), stop=(oc == NCH - 1))
            cp.sig(t.matmul(out=psTf[0:1, 0:512], lhsT=onesf[:, 0:1], rhs=scr[2][:, 512:1024],
                            start=(oc == 0), stop=(oc == NCH - 1)))
        em("pe", f)

        def f(v, oc=oc):
            cp.wait(v)
            ca.wait(v, ev_silu)
            v.scalar_tensor_tensor(out=scr[2][:, 0:TPC], in0=scr[3][:, 0:TPC],
                                   scalar=col(gwcols, oc), in1=fbuf[:, TPC * oc:TPC * (oc + 1)],
                                   op0=OP.mult, op1=OP.mult)
            cv.sig(v.tensor_copy(out=fbuf[:, TPC * oc:TPC * (oc + 1)], in_=scr[2][:, 0:TPC]))
        em("dve", f)
    emit_fm_matmul("g", evac_g)
    dbg_tap("opre0", fbuf[:, 0:TPC], gate=cv)

    def f(s):
        cp.wait(s)
        s.copy(out=gssr[0:1, 0:512], in_=psS[0:1, 0:512])
        ca.sig(s.copy(out=gssr[0:1, 512:1024], in_=psTf[0:1, 0:512]))
    em("act", f)
    ev_gss = ca.n
    dbg_tap("gss", gssr[:], gate=ca)

    def f(sy):
        ca.wait(sy)
        cv.wait(sy)
        clw.dma(sy.dma_start(out=cst[:], in_=nrm_d["o"][:]))
    em("sync", f)

    for t in range(NT):
        for q in range(4):
            def f(te, t=t, q=q):
                cv.wait(te)
                ca.wait(te)
                for j in range(4):
                    c = 4 * q + j
                    ins = te.transpose(out=psTf[:, 128 * j:128 * (j + 1)],
                                       in_=fbuf[:, TPC * c + 128 * t:TPC * c + 128 * (t + 1)],
                                       identity=idf[:])
                cp.sig(ins)
            em("pe", f)

            def f(s, q=q):
                cp.wait(s)
                ca.sig(s.copy(out=scr[0][:, 512 * q:512 * (q + 1)], in_=psTf[:, 0:512]))
            em("act", f)

        def f(te, t=t):
            ca.wait(te)
            cp.sig(te.matmul(out=psTf[:, 0:1], lhsT=gssr[0:1, 128 * t:128 * (t + 1)],
                             rhs=onesf[0:1, 0:1], start=True, stop=True))
        em("pe", f)

        def f(s, t=t):
            cp.wait(s)
            cv.wait(s)
            s.copy(out=col(stats, ST_T0 + 6), in_=psTf[:, 0:1])
            ssp(s)
            s.activation(out=col(stats, ST_T0 + 7), in_=col(stats, ST_T0 + 6),
                         func=AF.Ln, scale=1.0 / float(D), bias=col(biasc, 1))
            ssp(s)
            s.activation(out=col(stats, ST_GRS + t), in_=col(stats, ST_T0 + 7),
                         func=AF.Exp, scale=-0.5)
            s.activation(out=scr[2][:], in_=scr[0][:], func=AF.Square,
                         accum_out=col(stats, ST_T0 + 8))
            ssp(s)
            ca.sig(s.copy(out=junk[:, 2:3], in_=zeros[:, 0:1]))
        em("act", f)

        def f(v, t=t):
            ca.wait(v)
            TS(v, col(stats, ST_T0 + 9), col(stats, ST_T0 + 8), col(stats, ST_GRS + t),
               col(stats, ST_GRS + t), OP.mult, OP.mult)
            cv.sig(v.tensor_copy(out=junk[:, 4:5], in_=zeros[:, 0:1]))
        em("dve", f)

        def f(s, t=t):
            cv.wait(s)
            s.activation(out=col(stats, ST_T0 + 10), in_=col(stats, ST_T0 + 9),
                         func=AF.Ln, scale=1.0 / float(D), bias=col(biasc, 0))
            ssp(s)
            ca.sig(s.activation(out=col(stats, ST_T0 + 11), in_=col(stats, ST_T0 + 10),
                                func=AF.Exp, scale=-0.5))
        em("act", f)

        def f(v, t=t):
            ca.wait(v)
            clw.wait(v)
            v.tensor_tensor(out=scr[1][:], in0=scr[0][:], in1=cst[:], op=OP.mult)
            v.tensor_reduce(out=col(stats, ST_T0), in_=scr[1][:], axis=AX,
                            op=OP.max, apply_absolute_value=True)
            vsp(v)
            TS(v, col(stats, ST_T0 + 1), col(stats, ST_GRS + t), col(stats, ST_T0 + 11))
            vsp(v)
            TS(v, col(stats, ST_T0 + 2), col(stats, ST_T0), col(stats, ST_T0 + 1),
               1e-5, OP.mult, OP.max)
            vsp(v)
            v.reciprocal(out=col(stats, ST_T0 + 3), in_=col(stats, ST_T0 + 2))
            vsp(v)
            TS(v, col(stats, ST_T0 + 4), col(stats, ST_T0 + 3), col(stats, ST_T0 + 1),
               127.0, OP.mult, OP.mult)
            TS(v, col(stats, 104 + t), col(stats, ST_T0 + 2), col(wsc, 4 + 3),
               1.0 / 127.0, OP.mult, OP.mult)
            TS(v, scr[2][:], scr[1][:], col(stats, ST_T0 + 4))
            cv.sig(TS(v, scb[0][:], scr[2][:], MAGIC, MAGIC, OP.add, OP.subtract))
        em("dve", f)
        if t == 0:
            dbg_tap("ko0", scb[0][:], gate=cv)

        def f(te):
            cv.wait(te)
        em("pe", f)
        emit_transpose16_bf(scb[0][:],
                            lambda c, t=t: kbuf[:, TPC * c + 128 * t:TPC * c + 128 * (t + 1)])

    for osp in range(4):
        def f(sy, osp=osp):
            cp.wait(sy)
            for rr in range(2):
                r = 2 * osp + rr
                src = ag1_out[D * r:D * (r + 1), RS * 3:RS * 4]
                cls.dma(sy.dma_start(
                    out=wslab[:].rearrange("p (k j) -> p k j", j=512)[:, :, 256 * rr:256 * (rr + 1)],
                    in_=src.rearrange("(k p) j -> p k j", p=128)))
        em("sync", f)
        for t in range(NT):
            def f(sy, osp=osp, t=t):
                cv.wait(sy)
                clx.dma(sy.dma_start(out=scr[1][:, 0:512],
                                     in_=x_d[128 * t:128 * (t + 1), 512 * osp:512 * (osp + 1)]))
            em("sync", f)

            def f(te, osp=osp, t=t):
                cls.wait(te)
                clx.wait(te)
                cv.wait(te)
                ca.wait(te)
                ps = psA if t % 2 == 0 else psB
                for k in range(NCH):
                    ins = te.matmul(out=ps[:, 0:512],
                                    lhsT=kbuf[:, TPC * k + 128 * t:TPC * k + 128 * (t + 1)],
                                    rhs=wslab[:, 512 * k:512 * (k + 1)],
                                    start=(k == 0), stop=(k == NCH - 1))
                cp.sig(ins)
            em("pe", f)

            def f(v, osp=osp, t=t):
                cp.wait(v)
                ps = psA if t % 2 == 0 else psB
                cv.sig(v.scalar_tensor_tensor(
                    out=fbuf[:, D * t + 512 * osp:D * t + 512 * (osp + 1)],
                    in0=ps[:, 0:512], scalar=col(stats, 104 + t),
                    in1=scr[1][:, 0:512], op0=OP.mult, op1=OP.add))
            em("dve", f)
    dbg_tap("y1t0", fbuf[:, 0:D], gate=cv)

    for t in range(NT):
        def f(v, t=t):
            vsp(v)
            y1 = fbuf[:, D * t:D * (t + 1)]
            v.tensor_reduce(out=col(stats, ST_T0), in_=y1, axis=AX, op=OP.add)
            vsp(v)
            TS(v, col(stats, ST_MU + t), col(stats, ST_T0), 1.0 / float(D))
            vsp(v)
            cv.sig(TS(v, scr[0][:], y1, col(stats, ST_MU + t), None, OP.subtract))
        em("dve", f)

        def f(s, t=t):
            cv.wait(s)
            s.activation(out=scr[2][:], in_=scr[0][:], func=AF.Square,
                         accum_out=col(stats, ST_T0 + 1))
            ssp(s)
            s.activation(out=col(stats, ST_T0 + 2), in_=col(stats, ST_T0 + 1),
                         func=AF.Ln, scale=1.0 / float(D), bias=col(biasc, 1))
            ssp(s)
            ca.sig(s.activation(out=col(stats, ST_RSN + t), in_=col(stats, ST_T0 + 2),
                                func=AF.Exp, scale=-0.5))
        em("act", f)

        def f(v, t=t):
            ca.wait(v)
            TS(v, fbuf[:, D * t:D * (t + 1)], scr[0][:], col(stats, ST_RSN + t))
            cv.sig(TS(v, scb[0][:], scr[0][:], col(stats, ST_RSN + t)))
        em("dve", f)
        if t == 0:
            dbg_tap("xln0", fbuf[:, 0:D], gate=cv)
        if t == 1:
            dbg_tap("xlnt1", fbuf[:, D:2 * D], gate=cv)

        def f(te):
            cv.wait(te)
        em("pe", f)
        emit_transpose16_bf(scb[0][:],
                            lambda c, t=t: kbuf[:, TPC * c + 128 * t:TPC * c + 128 * (t + 1)])

    for grp in range(4):
        for hc in range(NCH):
            if hc % 2 == 0:
                def f(sy, grp=grp, hc=hc):
                    cp.wait(sy)
                    cc.wait(sy, ev_ag2)
                    half = (hc // 2) % 2
                    r = 2 * grp + hc // 8
                    hl = 256 * ((hc // 2) % 4)
                    src = ag2a_out[D * r:D * (r + 1), hl:hl + 256]
                    cls.dma(sy.dma_start(
                        out=wslab[:, 4096 * half:4096 * half + 4096].rearrange(
                            "p (k j) -> p k j", j=256),
                        in_=src.rearrange("(k p) j -> p k j", p=128)))
                em("sync", f)
            half = (hc // 2) % 2
            ps = psA if hc % 2 == 0 else psB
            def f(te, hc=hc, half=half, ps=ps):
                cls.wait(te)
                cv.wait(te)
                ca.wait(te)
                for cpos in range(2):
                    for k in range(NCH):
                        ins = te.matmul(
                            out=ps[:, 512 * cpos:512 * (cpos + 1)],
                            lhsT=wslab[:, 4096 * half + 256 * k + 128 * (hc % 2):
                                       4096 * half + 256 * k + 128 * (hc % 2) + 128],
                            rhs=kbuf[:, TPC * k + 512 * cpos:TPC * k + 512 * (cpos + 1)],
                            start=(k == 0), stop=(k == NCH - 1))
                cp.sig(ins)
            em("pe", f)

            def f(s, ps=ps):
                cp.wait(s)
                cv.wait(s)
                ca.sig(s.activation(out=scr[2][:, 0:TPC], in_=ps[:, 0:TPC], func=AF.Relu))
            em("act", f)

            def f(v, hc=hc):
                ca.wait(v)
                cv.sig(v.tensor_tensor(out=ibuf[:, TPC * hc:TPC * (hc + 1)],
                                       in0=scr[2][:, 0:TPC], in1=scr[2][:, 0:TPC], op=OP.mult))
            em("dve", f)
        if grp == 0:
            dbg_tap("hh0", ibuf[:, 0:TPC], gate=cv)
        if grp == 3:
            dbg_tap("hh31", ibuf[:, TPC:2 * TPC], gate=cv)
        dbg_tap("ibg%d" % grp, ibuf[:], gate=cv)

        for osp in range(4):
            def f(sy, grp=grp, osp=osp):
                cp.wait(sy)
                for rr in range(2):
                    r = 2 * osp + rr
                    src = ag2b_out[H * r + D * grp:H * r + D * (grp + 1), 0:256]
                    cls.dma(sy.dma_start(
                        out=wslab[:].rearrange("p (k j) -> p k j", j=512)[:, :, 256 * rr:256 * (rr + 1)],
                        in_=src.rearrange("(k p) j -> p k j", p=128)))
            em("sync", f)
            for t in range(NT):
                def f(te, osp=osp, t=t):
                    cls.wait(te)
                    cv.wait(te)
                    ca.wait(te)
                    ps = psA if t % 2 == 0 else psB
                    for k in range(NCH):
                        ins = te.matmul(out=ps[:, 0:512],
                                        lhsT=ibuf[:, TPC * k + 128 * t:TPC * k + 128 * (t + 1)],
                                        rhs=wslab[:, 512 * k:512 * (k + 1)],
                                        start=(k == 0), stop=(k == NCH - 1))
                    cp.sig(ins)
                em("pe", f)

                def f(v, osp=osp, t=t):
                    cp.wait(v)
                    ps = psA if t % 2 == 0 else psB
                    dst = fbuf[:, D * t + 512 * osp:D * t + 512 * (osp + 1)]
                    cv.sig(v.tensor_tensor(out=dst, in0=dst, in1=ps[:, 0:512], op=OP.add))
                em("dve", f)

    dbg_tap("v2t1", fbuf[:, D:2 * D], gate=cv)

    for t in range(NT):
        def f(v, t=t):
            vsp(v)
            y2 = fbuf[:, D * t:D * (t + 1)]
            v.tensor_reduce(out=col(stats, ST_T0), in_=y2, axis=AX, op=OP.add)
            vsp(v)
            TS(v, col(stats, ST_MU2 + t), col(stats, ST_T0), 1.0 / float(D))
            vsp(v)
            cv.sig(TS(v, scr[0][:], y2, col(stats, ST_MU2 + t), None, OP.subtract))
        em("dve", f)

        def f(s, t=t):
            cv.wait(s)
            s.activation(out=scr[2][:], in_=scr[0][:], func=AF.Square,
                         accum_out=col(stats, ST_T0 + 1))
            ssp(s)
            s.activation(out=col(stats, ST_T0 + 2), in_=col(stats, ST_T0 + 1),
                         func=AF.Ln, scale=1.0 / float(D), bias=col(biasc, 1))
            ssp(s)
            ca.sig(s.activation(out=col(stats, ST_RSN2 + t), in_=col(stats, ST_T0 + 2),
                                func=AF.Exp, scale=-0.5))
        em("act", f)

        def f(v, t=t):
            ca.wait(v)
            cso.wait(v)
            cv.sig(TS(v, scr[1][:], scr[0][:], col(stats, ST_RSN2 + t)))
        em("dve", f)

        def f(sy, t=t):
            cv.wait(sy)
            cso.dma(sy.dma_start(out=y_d[128 * t:128 * (t + 1), :], in_=scr[1][:]))
        em("sync", f)

    def f(sy):
        cso.wait(sy)
    em("sync", f)

    block.__exit__(None, None, None)
    return nc


_prog_cache = {}


def _get_prog(dbg_names=()):
    key = tuple(dbg_names)
    if key not in _prog_cache:
        _prog_cache[key] = build_program(key)
    return _prog_cache[key]


def make_in_maps(inputs):
    x = np.ascontiguousarray(np.asarray(inputs["L0_attn_hidden_states"],
                                        dtype=np.float32).reshape(NCORES * TPC, D))
    W = {p: np.asarray(inputs["L0_attn_W" + p], np.float32) for p in ("i", "f", "g", "o")}
    wfc = np.asarray(inputs["L0_ffn_w_fc"], np.float32)
    wpj = np.asarray(inputs["L0_ffn_w_proj"], np.float32)
    nrm = {p: np.ascontiguousarray(np.broadcast_to(
        np.asarray(inputs["L0_attn_norm_" + p], np.float32), (128, D)))
        for p in ("i", "f", "g", "o")}
    gw = np.asarray(inputs["L0_attn_g_norm_w"], np.float32)
    gwcols = np.ascontiguousarray(gw.reshape(NCH, 128).T)
    in_maps = []
    for c in range(NCORES):
        sm = np.zeros((128, 16), np.float32)
        pos = c % 4
        base = 4 * (c // 4)
        for r in range(8):
            m = 1.0 if (base <= r < base + pos) else 0.0
            sm[:, r] = m
            sm[:, 8 + r] = 1.0 - m
        im = {
            "x": np.ascontiguousarray(x[TPC * c:TPC * (c + 1)]),
            "wfc": np.ascontiguousarray(wfc[FCS * c:FCS * (c + 1)]),
            "wpj": np.ascontiguousarray(wpj[RS * c:RS * (c + 1)]),
            "gwcols": gwcols,
            "smask": sm,
        }
        for p in ("i", "f", "g", "o"):
            im["w" + p] = np.ascontiguousarray(W[p][RS * c:RS * (c + 1)])
            im["nrm" + p] = nrm[p]
        in_maps.append(im)
    return in_maps


def kernel(**inputs):
    nc = _get_prog(_DBG)
    in_maps = make_in_maps(inputs)
    res = run_bass_kernel_spmd(nc, in_maps, core_ids=list(range(NCORES)))
    y = np.concatenate([r["y"] for r in res.results], axis=0)
    out = np.ascontiguousarray(y.reshape(2, 4096, D).astype(np.float32))
    if _DBG:
        kernel.dbg = [{k: v for k, v in r.items() if k.startswith("dbg_")}
                      for r in res.results]
    return out


# version 08
# speedup vs baseline: 1.1170x; 1.1170x over previous
import numpy as np
import ml_dtypes
import concourse.bass as bass
import concourse.mybir as mybir
from concourse.bass_utils import run_bass_kernel_spmd
from concourse.replica_groups import maybe_share_collective_output_space

F32 = mybir.dt.float32
BF16 = mybir.dt.bfloat16
AF = mybir.ActivationFunctionType
OP = mybir.AluOpType
AX = mybir.AxisListType.X
MAGIC = float(np.float32(1.5 * 2**23))

NCORES = 8
D = 2048
H = 8192
TPC = 1024
NT = 8
NCH = 16
RS = 256
FCS = 1024
SHRINKS = [float(np.float32(s)) for s in np.linspace(0.4, 1.0, 16).astype(np.float32)]
NSH = 16
QB = 64
NB = 32

_DBG = ()
_DBG_SHAPES = {
    "wsc": ([128, 16], F32), "kb0": ([128, 2048], BF16), "cb0": ([128, 64], F32),
    "fch0": ([128, TPC], F32), "iech0": ([128, TPC], BF16), "seg": ([128, 32], F32),
    "hin": ([128, 16], F32), "hch0": ([128, TPC], F32), "gss": ([1, TPC], F32),
    "opre0": ([128, TPC], F32), "y1t0": ([128, D], F32), "xln0": ([128, D], F32),
    "xlnt1": ([128, D], F32), "v2t1": ([128, D], F32), "hh31": ([128, TPC], BF16),
    "ibg0": ([128, 16 * TPC], BF16), "ibg1": ([128, 16 * TPC], BF16),
    "ibg2": ([128, 16 * TPC], BF16), "ibg3": ([128, 16 * TPC], BF16),
    "hh0": ([128, TPC], BF16), "u0": ([128, D], F32), "err0": ([128, NSH * NB], F32),
    "ko0": ([128, D], BF16), "stw": ([128, 4], F32), "row8": ([1, 8], F32),
}


class Ctr:
    def __init__(self, nc, name):
        self.h = nc.semaphore(name).__enter__()
        self.n = 0

    def sig(self, instr, k=1):
        instr.then_inc(self.h, k)
        self.n += k
        return self.n

    def dma(self, instr):
        return self.sig(instr, 16)

    def wait(self, eng, val=None):
        v = self.n if val is None else val
        assert v <= self.n
        if v > 0:
            eng.wait_ge(self.h, v)


def r3(ap, k):
    return ap.rearrange("p (b k) -> p b k", k=k)


def build_program(dbg_names=()):
    nc = bass.Bass(target_bir_lowering=False, detect_race_conditions=False)

    x_d = nc.dram_tensor("x", [TPC, D], F32, kind="ExternalInput").ap()
    w_d = {p: nc.dram_tensor("w" + p, [RS, D], F32, kind="ExternalInput").ap()
           for p in ("i", "f", "g", "o")}
    wfc_d = nc.dram_tensor("wfc", [FCS, D], F32, kind="ExternalInput").ap()
    wpj_d = nc.dram_tensor("wpj", [RS, H], F32, kind="ExternalInput").ap()
    nrm_d = {p: nc.dram_tensor("nrm" + p, [128, D], F32, kind="ExternalInput").ap()
             for p in ("i", "f", "g", "o")}
    gwc_d = nc.dram_tensor("gwcols", [128, NCH], F32, kind="ExternalInput").ap()
    smask_d = nc.dram_tensor("smask", [128, 16], F32, kind="ExternalInput").ap()
    y_d = nc.dram_tensor("y", [TPC, D], F32, kind="ExternalOutput").ap()

    ident_f = nc.inline_tensor(np.eye(128, dtype=np.float32), name="ident_f").ap()
    ident_b = nc.inline_tensor(np.eye(128, dtype=ml_dtypes.bfloat16), name="ident_b").ap()
    ones_f = nc.inline_tensor(np.ones((128, 128), np.float32), name="ones_f").ap()
    bias_np = np.zeros((128, 2), np.float32); bias_np[:, 0] = 1e-8; bias_np[:, 1] = 1e-5
    bias_d = nc.inline_tensor(bias_np, name="bias_c").ap()

    RG = [list(range(NCORES))]
    sp_ar = maybe_share_collective_output_space("AllReduce", RG)
    sp_ag = maybe_share_collective_output_space("AllGather", RG)
    ag0_in = nc.dram_tensor("ag0_in", [1, 8], F32, kind="Internal").ap()
    ag0_out = nc.dram_tensor("ag0_out", [1, 8], F32, kind="Internal", addr_space=sp_ar).ap()
    ag1_in = nc.dram_tensor("ag1_in", [D, 4 * RS], BF16, kind="Internal").ap()
    ag1_out = nc.dram_tensor("ag1_out", [NCORES * D, 4 * RS], BF16, kind="Internal", addr_space=sp_ag).ap()
    ag2a_in = nc.dram_tensor("ag2a_in", [D, FCS], BF16, kind="Internal").ap()
    ag2a_out = nc.dram_tensor("ag2a_out", [NCORES * D, FCS], BF16, kind="Internal", addr_space=sp_ag).ap()
    ag2b_in = nc.dram_tensor("ag2b_in", [H, RS], BF16, kind="Internal").ap()
    ag2b_out = nc.dram_tensor("ag2b_out", [NCORES * H, RS], BF16, kind="Internal", addr_space=sp_ag).ap()
    ag3_in = nc.dram_tensor("ag3_in", [128, 32], F32, kind="Internal").ap()
    ag3_out = nc.dram_tensor("ag3_out", [NCORES * 128, 32], F32, kind="Internal", addr_space=sp_ag).ap()

    dbg = {nm: nc.dram_tensor("dbg_" + nm, _DBG_SHAPES[nm][0], _DBG_SHAPES[nm][1],
                              kind="ExternalOutput").ap() for nm in dbg_names}

    def S(name, shape, dt):
        return nc.sbuf_tensor(name, shape, dt).__enter__()

    kbuf = S("kbuf", [128, NCH * TPC], BF16)
    fbuf = S("fbuf", [128, NCH * TPC], F32)
    ibuf = S("ibuf", [128, NCH * TPC], BF16)
    scr = [S(f"scr{i}", [128, D], F32) for i in range(4)]
    scb = [S("scb0", [128, D], BF16)]
    cst = S("cst", [128, D], F32)
    wslab = S("wslab", [128, 8192], BF16)
    wsv = wslab[:, 0:2048]
    wsv2 = wslab[:, 2048:4096]
    zeros = S("zeros_t", [128, TPC], F32)
    idf = S("idf", [128, 128], F32)
    idb = S("idb", [128, 128], BF16)
    onesf = S("onesf", [128, 128], F32)
    gwcols = S("gwcols_s", [128, NCH], F32)
    smask = S("smask_s", [128, 16], F32)
    stats = S("stats", [128, 128], F32)
    errst = S("errst", [128, NSH * NB], F32)
    eqb = S("eqb", [128, 256], F32)
    eqc = S("eqc", [128, NSH * NB], F32)
    cbs = S("cbs", [128, 64], F32)
    gath = S("gath", [128, 256], F32)
    hin = S("hin", [128, NCH], F32)
    junk = S("junk_t", [128, 8], F32)
    wsc = S("wsc", [128, 16], F32)
    row8 = S("row8", [1, 8], F32)
    biasc = S("biasc", [128, 2], F32)
    gssr = S("gssr", [1, TPC], F32)

    ST_SS = 0
    ST_RSTD = 8
    ST_GRS = 24
    ST_MU = 48
    ST_RSN = 56
    ST_W = 64
    ST_T0 = 68
    ST_MU2 = 80
    ST_RSN2 = 88

    psA = nc.psum_tensor("psA", [128, TPC], F32).__enter__()
    psB = nc.psum_tensor("psB", [128, TPC], F32).__enter__()
    psTb = nc.psum_tensor("psTb", [128, 1024], BF16).__enter__()
    psTf = nc.psum_tensor("psTf", [128, 512], F32).__enter__()
    psS = nc.psum_tensor("psS", [128, 512], F32).__enter__()

    clx = Ctr(nc, "clx")
    clw = Ctr(nc, "clw")
    cls = Ctr(nc, "cls")
    cs0 = Ctr(nc, "cs0")
    cs1 = Ctr(nc, "cs1")
    cs2 = Ctr(nc, "cs2")
    cs3 = Ctr(nc, "cs3")
    cso = Ctr(nc, "cso")
    cg = Ctr(nc, "cg")
    cc = Ctr(nc, "cc")
    cp = Ctr(nc, "cp")
    cv = Ctr(nc, "cv")
    ca = Ctr(nc, "ca")
    cq = Ctr(nc, "cq")
    cpl = Ctr(nc, "cpl")

    block = nc.Block().__enter__()
    ENG = {"sync": "sync", "dve": "vector", "act": "scalar", "pe": "tensor", "pool": "gpsimd"}

    DATAPATH = {"tensor_copy", "tensor_tensor", "tensor_scalar", "tensor_reduce",
                "scalar_tensor_tensor", "tensor_tensor_scan", "reciprocal", "memset",
                "memzero", "activation", "copy", "mul", "add", "square", "sqrt",
                "tensor_add", "tensor_sub", "tensor_mul", "tensor_max", "select",
                "tensor_single_scalar", "tensor_relu", "reduce_max", "reduce_sum"}

    class DrainWrap:
        def __init__(self, eng):
            self._e = eng

        def __getattr__(self, n):
            a = getattr(self._e, n)
            if n in DATAPATH:
                def call(*args, _a=a, **kw):
                    ins = _a(*args, **kw)
                    need = True
                    try:
                        out = kw.get("out")
                        if out is None and args:
                            out = args[0]
                        sh = out.shape
                        free = 1
                        for d in sh[1:]:
                            free *= int(d)
                        need = free < 512 or kw.get("accum_out") is not None
                    except Exception:
                        need = True
                    if need:
                        return self._e.drain()
                    return ins
                return call
            return a

    def em(engine, fn, wrap=True):
        if wrap and engine in ("dve", "act", "pool"):
            getattr(block, ENG[engine])(lambda e: fn(DrainWrap(e)))
        else:
            getattr(block, ENG[engine])(fn)

    def vsp(v):
        v.tensor_copy(out=junk[:, 0:1], in_=zeros[:, 0:1])

    def ssp(s):
        pass

    def TS(v, out, in0, s1, s2=None, op0=OP.mult, op1=None):
        if op1 is None:
            return v.tensor_scalar(out=out, in0=in0, scalar1=s1, scalar2=None, op0=op0)
        return v.tensor_scalar(out=out, in0=in0, scalar1=s1, scalar2=s2, op0=op0, op1=op1)

    def col(t, c):
        return t[:, c:c + 1]

    def dbg_tap(name, src_ap, gate=None):
        if name in dbg:
            def f(sy, name=name, src=src_ap, gate=gate):
                if gate is not None:
                    gate.wait(sy)
                cso.dma(sy.dma_start(out=dbg[name][:], in_=src))
            em("sync", f)
            for e in ("dve", "act", "pe", "pool"):
                def g(eng):
                    cso.wait(eng)
                em(e, g)

    def emit_transpose16_bf(src_tile, dst_fn):
        for half in range(2):
            def f(t, half=half, src=src_tile):
                ca.wait(t)
                for j in range(8):
                    c = 8 * half + j
                    ins = t.transpose(out=psTb[:, 128 * j:128 * (j + 1)],
                                      in_=src[:, 128 * c:128 * (c + 1)], identity=idb[:])
                cp.sig(ins)
            em("pe", f)

            def f(s, half=half, dst_fn=dst_fn):
                cp.wait(s)
                for j in range(8):
                    c = 8 * half + j
                    ins = s.copy(out=dst_fn(c), in_=psTb[:, 128 * j:128 * (j + 1)])
                ca.sig(ins)
            em("act", f)

    def f(sy):
        clw.dma(sy.dma_start(out=idf[:], in_=ident_f[:]))
        clw.dma(sy.dma_start(out=idb[:], in_=ident_b[:]))
        clw.dma(sy.dma_start(out=onesf[:], in_=ones_f[:]))
        clw.dma(sy.dma_start(out=gwcols[:], in_=gwc_d[:]))
        clw.dma(sy.dma_start(out=smask[:], in_=smask_d[:]))
        clw.dma(sy.dma_start(out=biasc[:], in_=bias_d[:]))
    em("sync", f)

    def f(v):
        clw.wait(v)
        v.memset(zeros[:], 0.0)
        v.memset(junk[:], 0.0)
        v.memset(stats[:], 0.0)
        v.memset(eqb[:], 0.0)
        v.memset(errst[:], 0.0)
        v.memset(eqc[:], 0.0)
        v.memset(cbs[:], 0.0)
        v.memset(hin[:], 0.0)
        v.memset(gath[:], 0.0)
        v.memset(wsc[:], 0.0)
        cv.sig(v.memset(gssr[:], 0.0))
    em("dve", f)
    ev_consts = cv.n

    for wi, p in enumerate("ifgo"):
        def f(sy, p=p):
            cv.wait(sy)
            clw.dma(sy.dma_start(out=scr[0][:], in_=w_d[p][0:128, :]))
            clw.dma(sy.dma_start(out=scr[1][:], in_=w_d[p][128:256, :]))
        em("sync", f)

        def f(v, wi=wi):
            clw.wait(v)
            cv.wait(v, ev_consts)
            v.tensor_reduce(out=col(stats, ST_T0), in_=scr[0][:], axis=AX, op=OP.add,
                            apply_absolute_value=True)
            v.tensor_reduce(out=col(stats, ST_T0 + 1), in_=scr[1][:], axis=AX, op=OP.add,
                            apply_absolute_value=True)
            vsp(v)
            cv.sig(v.tensor_tensor(out=col(stats, ST_W + wi), in0=col(stats, ST_T0),
                                   in1=col(stats, ST_T0 + 1), op=OP.add))
        em("dve", f)
    ev_wparts = cv.n
    dbg_tap("stw", stats[:, ST_W:ST_W + 4], gate=cv)

    def f(t):
        cv.wait(t, ev_wparts)
        cp.sig(t.matmul(out=psS[0:1, 0:4], lhsT=onesf[:, 0:1],
                        rhs=stats[:, ST_W:ST_W + 4], start=True, stop=True))
    em("pe", f)
    ev_pp = cp.n

    def f(s):
        cp.wait(s, ev_pp)
        s.memzero(row8[:])
        ca.sig(s.copy(out=row8[0:1, 0:4], in_=psS[0:1, 0:4]))
    em("act", f)
    ev_row8 = ca.n

    def f(sy):
        ca.wait(sy, ev_row8)
        cs0.dma(sy.dma_start(out=ag0_in[:], in_=row8[:]))
    em("sync", f)
    ev_ag0st = cs0.n

    def f(g):
        cs0.wait(g, ev_ag0st)
        cc.sig(g.collective_compute("AllReduce", OP.add, replica_groups=RG,
                                    ins=[ag0_in[:].opt()], outs=[ag0_out[:].opt()]))
        cc.wait(g)
        cg.dma(g.dma_start(out=row8[:], in_=ag0_out[:]))
    em("pool", f)
    ev_ag0 = cg.n

    def f(v):
        cg.wait(v, ev_ag0)
        TS(v, row8[0:1, 4:8], row8[0:1, 0:4], 1.0 / float(D * D), 1e-5, OP.mult, OP.max)
        vsp(v)
        cv.sig(v.reciprocal(out=row8[0:1, 0:4], in_=row8[0:1, 4:8]))
    em("dve", f)
    ev_wsc1 = cv.n
    dbg_tap("row8", row8[:], gate=cv)

    def f(t):
        cv.wait(t, ev_wsc1)
        cp.sig(t.matmul(out=psS[:, 0:8], lhsT=onesf[0:1, :], rhs=row8[0:1, 0:8],
                        start=True, stop=True))
    em("pe", f)
    ev_wsc2 = cp.n

    def f(s):
        cp.wait(s, ev_wsc2)
        ca.sig(s.copy(out=wsc[:, 0:8], in_=psS[:, 0:8]))
    em("act", f)
    ev_wsc = ca.n
    dbg_tap("wsc", wsc[:], gate=ca)

    for wi, p in enumerate("ifgo"):
        for rt in range(2):
            def f(sy, p=p, rt=rt):
                cv.wait(sy)
                ca.wait(sy)
                cs1.wait(sy)
                clw.dma(sy.dma_start(out=scr[0][:], in_=w_d[p][128 * rt:128 * (rt + 1), :]))
            em("sync", f)

            def f(v, wi=wi):
                clw.wait(v)
                ca.wait(v, ev_wsc)
                TS(v, scr[1][:], scr[0][:], col(wsc, wi))
                TS(v, scr[2][:], scr[1][:], MAGIC, MAGIC, OP.add, OP.subtract)
                cv.sig(TS(v, scb[0][:], scr[2][:], 1.0, -1.0, OP.min, OP.max))
            em("dve", f)

            def f(t):
                cv.wait(t)
            em("pe", f)
            emit_transpose16_bf(scb[0][:], lambda c: wsv[:, 128 * c:128 * (c + 1)])

            def f(sy, wi=wi, rt=rt):
                ca.wait(sy)
                c0 = 256 * wi + 128 * rt
                cs1.dma(sy.dma_start(
                    out=ag1_in[:, c0:c0 + 128].rearrange("(c p) j -> p c j", p=128),
                    in_=r3(wsv, 128)))
            em("sync", f)
    ev_t_staged = cs1.n

    def f(g):
        cs1.wait(g, ev_t_staged)
        cc.sig(g.collective_compute("AllGather", OP.bypass, replica_groups=RG,
                                    ins=[ag1_in[:].opt()], outs=[ag1_out[:].opt()]))
    em("pool", f)
    ev_ag1 = cc.n

    ALPH = [float(a) for a in np.float32(np.linspace(0.4, 1.0, 16))]
    C15 = float(np.float32(np.float32(1.0) / np.float32(15.0)))
    wTs = (scr[0][:], fbuf[:, 10240:12288])

    def _qsrc(ti):
        if ti < 8:
            return wfc_d[128 * ti:128 * (ti + 1), :]
        rt, csl = (ti - 8) // 4, (ti - 8) % 4
        return wpj_d[128 * rt:128 * (rt + 1), 2048 * csl:2048 * (csl + 1)]

    for ti in range(16):
        wT = wTs[ti % 2]

        def f(sy, ti=ti):
            cv.wait(sy)
            ca.wait(sy)
            cs1.wait(sy)
            cs2.wait(sy)
            if ti == 0:
                clw.dma(sy.dma_start(out=wTs[0], in_=_qsrc(0)))
            if ti < 15:
                clw.dma(sy.dma_start(out=wTs[(ti + 1) % 2], in_=_qsrc(ti + 1)))
        em("sync", f)

        def f(v, wT=wT):
            clw.wait(v)
            v.tensor_reduce(out=eqb[:, 0:32], in_=r3(wT, QB), axis=AX,
                            op=OP.max, apply_absolute_value=True)
            vsp(v)
            cv.sig(TS(v, eqb[:, 32:64], eqb[:, 0:32], 1e-8, C15,
                      OP.max, OP.mult))
        em("dve", f)

        dT = (fbuf[:, 0:2048], fbuf[:, 2048:4096])
        qT3 = (fbuf[:, 4096:6144], fbuf[:, 6144:8192], fbuf[:, 8192:10240])
        SKC4 = (eqc[:, 0:32], eqc[:, 192:224], eqc[:, 256:288], eqc[:, 384:416])
        RKC2 = (eqc[:, 32:64], eqc[:, 224:256])
        BERR = eqc[:, 64:96]
        BSC = eqc[:, 96:128]
        DTMP2 = (eqc[:, 128:160], eqc[:, 352:384])
        DIF = (eqc[:, 160:192], eqc[:, 320:352])

        def sel_ops(v, m):
            pass

        for k in range(NSH):
            def f(v, k=k, wT=wT):
                j = k - 2
                ej = errst[:, 32 * j:32 * (j + 1)]
                if k == 0:
                    TS(v, SKC4[0], eqb[:, 32:64], ALPH[0])
                    v.drain()
                    v.reciprocal(out=RKC2[0], in_=SKC4[0])
                    v.drain()
                rke = r3(RKC2[k % 2], 1).broadcast_to([128, NB, QB])
                ske = r3(SKC4[k % 4], 1).broadcast_to([128, NB, QB])
                v.tensor_tensor(out=r3(scr[1][:], QB), in0=r3(wT, QB),
                                in1=rke, op=OP.mult)
                if k < NSH - 1:
                    TS(v, SKC4[(k + 1) % 4], eqb[:, 32:64], ALPH[k + 1])
                if k >= 2:
                    ca.wait(v)
                    v.tensor_reduce(out=ej, in_=r3(qT3[j % 3], QB),
                                    axis=AX, op=OP.add)
                if k >= 4:
                    v.tensor_tensor(out=DIF[(k - 3) % 2], in0=DIF[(k - 3) % 2],
                                    in1=DTMP2[(k - 3) % 2], op=OP.mult)
                TS(v, scr[2][:], scr[1][:], MAGIC, MAGIC, OP.add, OP.subtract)
                if k < NSH - 1:
                    v.reciprocal(out=RKC2[(k + 1) % 2], in_=SKC4[(k + 1) % 4])
                if k >= 4:
                    v.tensor_tensor(out=BSC, in0=BSC, in1=DIF[(k - 3) % 2],
                                    op=OP.add)
                TS(v, scr[2][:], scr[2][:], 15.0, -15.0, OP.min, OP.max)
                if k == 2:
                    v.tensor_copy(out=BERR, in_=errst[:, 0:32])
                if k >= 3:
                    v.tensor_tensor(out=DTMP2[j % 2], in0=ej, in1=BERR,
                                    op=OP.subtract)
                v.tensor_tensor(out=r3(scr[3][:], QB), in0=r3(scr[2][:], QB),
                                in1=ske, op=OP.mult)
                if k == 2:
                    v.tensor_copy(out=BSC, in_=SKC4[0])
                if k >= 3:
                    TS(v, DTMP2[j % 2], DTMP2[j % 2], 0.0, None, OP.is_lt)
                cv.sig(v.tensor_tensor(out=dT[k % 2], in0=scr[3][:],
                                       in1=wT, op=OP.subtract))
                if k >= 3:
                    v.tensor_tensor(out=DIF[j % 2], in0=SKC4[j % 4],
                                    in1=BSC, op=OP.subtract)
                    v.tensor_tensor(out=BERR, in0=BERR, in1=ej, op=OP.min)
            em("dve", f, wrap=False)

            def f(s, k=k):
                cv.wait(s)
                ca.sig(s.activation(out=qT3[k % 3], in_=dT[k % 2], func=AF.Square))
            em("act", f)

        def f(v, wT=wT):
            def wj():
                v.tensor_copy(out=scr[1][:, 0:512], in_=wT[:, 0:512])
            e14 = errst[:, 32 * 14:32 * 15]
            e15 = errst[:, 32 * 15:32 * 16]
            ca.wait(v, ca.n - 1)
            v.tensor_reduce(out=e14, in_=r3(qT3[14 % 3], QB), axis=AX, op=OP.add)
            wj()
            v.tensor_tensor(out=DIF[1], in0=DIF[1], in1=DTMP2[1], op=OP.mult)
            wj()
            v.tensor_tensor(out=BSC, in0=BSC, in1=DIF[1], op=OP.add)
            wj()
            v.tensor_tensor(out=DTMP2[0], in0=e14, in1=BERR, op=OP.subtract)
            v.drain()
            TS(v, DTMP2[0], DTMP2[0], 0.0, None, OP.is_lt)
            wj()
            v.tensor_tensor(out=DIF[0], in0=SKC4[14 % 4], in1=BSC, op=OP.subtract)
            v.drain()
            v.tensor_tensor(out=DIF[0], in0=DIF[0], in1=DTMP2[0], op=OP.mult)
            wj()
            v.tensor_tensor(out=BERR, in0=BERR, in1=e14, op=OP.min)
            ca.wait(v)
            v.tensor_reduce(out=e15, in_=r3(qT3[15 % 3], QB), axis=AX, op=OP.add)
            wj()
            v.tensor_tensor(out=BSC, in0=BSC, in1=DIF[0], op=OP.add)
            wj()
            v.tensor_tensor(out=DTMP2[1], in0=e15, in1=BERR, op=OP.subtract)
            v.drain()
            TS(v, DTMP2[1], DTMP2[1], 0.0, None, OP.is_lt)
            wj()
            v.tensor_tensor(out=DIF[1], in0=SKC4[15 % 4], in1=BSC, op=OP.subtract)
            v.drain()
            v.tensor_tensor(out=DIF[1], in0=DIF[1], in1=DTMP2[1], op=OP.mult)
            wj()
            v.tensor_tensor(out=BSC, in0=BSC, in1=DIF[1], op=OP.add)
            cv.sig(v.drain())
        em("dve", f, wrap=False)
        if ti == 0:
            dbg_tap("err0", errst[:], gate=cv)

        def f(v, wT=wT):
            sb = eqc[:, 96:128]
            vsp(v)
            v.reciprocal(out=eqc[:, 0:32], in_=sb)
            vsp(v)
            v.tensor_copy(out=cbs[:, 32:64], in_=sb)
            rse = r3(eqc[:, 0:32], 1).broadcast_to([128, NB, QB])
            sbe = r3(sb, 1).broadcast_to([128, NB, QB])
            v.tensor_tensor(out=r3(scr[1][:], QB), in0=r3(wT, QB),
                            in1=rse, op=OP.mult)
            TS(v, scr[2][:], scr[1][:], MAGIC, MAGIC, OP.add, OP.subtract)
            TS(v, scr[2][:], scr[2][:], 15.0, -15.0, OP.min, OP.max)
            v.tensor_tensor(out=r3(scr[3][:], QB), in0=r3(scr[2][:], QB),
                            in1=sbe, op=OP.mult)
            cv.sig(v.tensor_copy(out=wsv2, in_=scr[3][:]))
        em("dve", f)
        if ti == 0:
            dbg_tap("cb0", cbs[:], gate=cv)

        def f(t):
            cv.wait(t)
        em("pe", f)
        emit_transpose16_bf(wsv2, lambda c: wsv[:, 128 * c:128 * (c + 1)])

        def f(sy, ti=ti):
            ca.wait(sy)
            if ti < 8:
                dst = ag2a_in[:, 128 * ti:128 * (ti + 1)]
            else:
                rt, csl = (ti - 8) // 4, (ti - 8) % 4
                dst = ag2b_in[2048 * csl:2048 * (csl + 1), 128 * rt:128 * (rt + 1)]
            cs2.dma(sy.dma_start(out=dst.rearrange("(c p) j -> p c j", p=128),
                                 in_=r3(wsv, 128)))
        em("sync", f)
    ev_ffn_staged = cs2.n

    def f(g):
        cs2.wait(g, ev_ffn_staged)
        cc.sig(g.collective_compute("AllGather", OP.bypass, replica_groups=RG,
                                    ins=[ag2a_in[:].opt()], outs=[ag2a_out[:].opt()]))
        cc.sig(g.collective_compute("AllGather", OP.bypass, replica_groups=RG,
                                    ins=[ag2b_in[:].opt()], outs=[ag2b_out[:].opt()]))
    em("pool", f)
    ev_ag2 = cc.n

    W_IDX = {"i": 0, "f": 1, "g": 2, "o": 3}

    def emit_stageA(p, with_ss):
        def f(sy, p=p):
            cv.wait(sy)
            ca.wait(sy)
            clw.dma(sy.dma_start(out=cst[:], in_=nrm_d[p][:]))
        em("sync", f)
        for t in range(NT):
            def f(sy, t=t):
                cv.wait(sy)
                ca.wait(sy)
                clx.dma(sy.dma_start(out=scr[0][:], in_=x_d[128 * t:128 * (t + 1), :]))
            em("sync", f)
            if with_ss:
                def f(s, t=t):
                    clx.wait(s)
                    cv.wait(s)
                    s.activation(out=scr[2][:], in_=scr[0][:], func=AF.Square,
                                 accum_out=col(stats, ST_SS + t))
                    ssp(s)
                    s.activation(out=col(stats, ST_T0 + 2), in_=col(stats, ST_SS + t),
                                 func=AF.Ln, scale=1.0 / float(D), bias=col(biasc, 0))
                    ssp(s)
                    ca.sig(s.activation(out=col(stats, ST_RSTD + t), in_=col(stats, ST_T0 + 2),
                                        func=AF.Exp, scale=-0.5))
                em("act", f)

            def f(v, t=t, p=p):
                clx.wait(v)
                clw.wait(v)
                ca.wait(v)
                wi = W_IDX[p]
                v.tensor_tensor(out=scr[1][:], in0=scr[0][:], in1=cst[:], op=OP.mult)
                vsp(v)
                v.tensor_reduce(out=col(stats, ST_T0), in_=scr[1][:], axis=AX,
                                op=OP.max, apply_absolute_value=True)
                vsp(v)
                TS(v, col(stats, ST_T0 + 1), col(stats, ST_T0), col(stats, ST_RSTD + t),
                   1e-5, OP.mult, OP.max)
                vsp(v)
                v.reciprocal(out=col(stats, ST_T0 + 3), in_=col(stats, ST_T0 + 1))
                vsp(v)
                TS(v, col(stats, ST_T0 + 4), col(stats, ST_T0 + 3), col(stats, ST_RSTD + t),
                   127.0, OP.mult, OP.mult)
                TS(v, col(stats, 96 + t), col(stats, ST_T0 + 1), col(wsc, 4 + wi),
                   1.0 / 127.0, OP.mult, OP.mult)
                TS(v, scr[2][:], scr[1][:], col(stats, ST_T0 + 4))
                cv.sig(TS(v, scb[0][:], scr[2][:], MAGIC, MAGIC, OP.add, OP.subtract))
            em("dve", f)

            def f(te):
                cv.wait(te)
            em("pe", f)
            emit_transpose16_bf(scb[0][:],
                                lambda c, t=t: kbuf[:, TPC * c + 128 * t:TPC * c + 128 * (t + 1)])

    def emit_cexp():
        crow = scr[1][0:1, TPC:TPC + TPC]
        for t in range(NT):
            def f(te, t=t):
                cv.wait(te)
                ca.wait(te)
                cp.sig(te.matmul(out=psTf[0:1, 0:128], lhsT=col(stats, 96 + t),
                                 rhs=idf[:], start=True, stop=True))
            em("pe", f)

            def f(sz, t=t, crow=crow):
                cp.wait(sz)
                ca.sig(sz.copy(out=crow[0:1, 128 * t:128 * (t + 1)], in_=psTf[0:1, 0:128]))
            em("act", f)

        def f(te, crow=crow):
            ca.wait(te)
            te.matmul(out=psB[:, 0:512], lhsT=onesf[0:1, :], rhs=crow[0:1, 0:512],
                      start=True, stop=True)
            cp.sig(te.matmul(out=psB[:, 512:1024], lhsT=onesf[0:1, :],
                             rhs=crow[0:1, 512:1024], start=True, stop=True))
        em("pe", f)

        def f(sz):
            cp.wait(sz)
            ca.sig(sz.copy(out=scr[1][:, 0:TPC], in_=psB[:, 0:TPC]))
        em("act", f)

    def emit_fm_matmul(p, evac):
        wi = W_IDX[p]
        emit_cexp()
        for oc in range(NCH):
            if oc % 2 == 0:
                r = oc // 2
                half = r % 2
                def f(sy, r=r, half=half, wi=wi):
                    cp.wait(sy)
                    cs1.wait(sy)
                    cs2.wait(sy)
                    cc.wait(sy, ev_ag1)
                    src = ag1_out[D * r:D * (r + 1), RS * wi:RS * (wi + 1)]
                    cls.dma(sy.dma_start(
                        out=wslab[:, 4096 * half:4096 * half + 4096].rearrange(
                            "p (k j) -> p k j", j=256),
                        in_=src.rearrange("(k p) j -> p k j", p=128)))
                em("sync", f)
            half = (oc // 2) % 2
            ps = psA if oc % 2 == 0 else psB
            def f(t, oc=oc, half=half, ps=ps):
                cls.wait(t)
                cv.wait(t)
                ca.wait(t)
                for cpos in range(2):
                    for k in range(NCH):
                        ins = t.matmul(
                            out=ps[:, 512 * cpos:512 * (cpos + 1)],
                            lhsT=wslab[:, 4096 * half + 256 * k + 128 * (oc % 2):
                                       4096 * half + 256 * k + 128 * (oc % 2) + 128],
                            rhs=kbuf[:, TPC * k + 512 * cpos:TPC * k + 512 * (cpos + 1)],
                            start=(k == 0), stop=(k == NCH - 1))
                cp.sig(ins)
            em("pe", f)
            evac(oc, ps)

    emit_stageA("f", with_ss=True)

    def evac_f(oc, ps):
        def f(v, oc=oc, ps=ps):
            cp.wait(v)
            ca.wait(v)
            cv.sig(v.tensor_tensor(out=scr[2][:, 0:TPC], in0=ps[:, 0:TPC],
                                   in1=scr[1][:, 0:TPC], op=OP.mult))
        em("dve", f)

        def f(s, oc=oc):
            cv.wait(s)
            ca.sig(s.activation(out=fbuf[:, TPC * oc:TPC * (oc + 1)],
                                in_=scr[2][:, 0:TPC], func=AF.Sigmoid))
        em("act", f)
    emit_fm_matmul("f", evac_f)
    dbg_tap("fch0", fbuf[:, 0:TPC], gate=ca)

    emit_stageA("i", with_ss=False)

    def evac_i(oc, ps):
        def f(v, oc=oc, ps=ps):
            cp.wait(v)
            ca.wait(v)
            cv.sig(v.tensor_tensor(out=scr[3][:, 0:TPC], in0=ps[:, 0:TPC],
                                   in1=scr[1][:, 0:TPC], op=OP.mult))
        em("dve", f)

        def f(s, oc=oc):
            cv.wait(s)
            ca.sig(s.activation(out=scr[2][:, 0:TPC], in_=scr[3][:, 0:TPC], func=AF.Silu))
        em("act", f)

        def f(v, oc=oc):
            ca.wait(v)
            TS(v, scr[3][:, 0:TPC], fbuf[:, TPC * oc:TPC * (oc + 1)], -1.0, 1.0,
               OP.mult, OP.add)
            cv.sig(v.tensor_tensor(out=ibuf[:, TPC * oc:TPC * (oc + 1)],
                                   in0=scr[2][:, 0:TPC], in1=scr[3][:, 0:TPC], op=OP.mult))
        em("dve", f)
    emit_fm_matmul("i", evac_i)
    dbg_tap("iech0", ibuf[:, 0:TPC], gate=cv)

    for c in range(NCH):
        def f(v, c=c):
            if c == 0:
                ca.wait(v)
            v.tensor_tensor_scan(out=scr[2][:, 0:TPC], data0=fbuf[:, TPC * c:TPC * (c + 1)],
                                 data1=zeros[:, 0:TPC], initial=1.0, op0=OP.mult, op1=OP.add)
            v.tensor_tensor_scan(out=scr[3][:, 0:TPC], data0=fbuf[:, TPC * c:TPC * (c + 1)],
                                 data1=ibuf[:, TPC * c:TPC * (c + 1)], initial=0.0,
                                 op0=OP.mult, op1=OP.add)
            v.tensor_copy(out=col(cbs, c), in_=scr[2][:, TPC - 1:TPC])
            ins = v.tensor_copy(out=col(cbs, 16 + c), in_=scr[3][:, TPC - 1:TPC])
            if c == NCH - 1:
                cv.sig(ins)
        em("dve", f)
    dbg_tap("seg", cbs[:, 0:32], gate=cv)

    def f(sy):
        cv.wait(sy)
        cs3.dma(sy.dma_start(out=ag3_in[:], in_=cbs[:, 0:32]))
    em("sync", f)
    ev_ag3st = cs3.n

    def f(g):
        cs3.wait(g, ev_ag3st)
        cc.sig(g.collective_compute("AllGather", OP.bypass, replica_groups=RG,
                                    ins=[ag3_in[:].opt()], outs=[ag3_out[:].opt()]))
        cc.wait(g)
        cg.dma(g.dma_start(out=gath[:].rearrange("p (g j) -> p g j", j=32),
                           in_=ag3_out[:].rearrange("(g p) j -> p g j", p=128)))
    em("pool", f)
    ev_gath = cg.n

    def f(v):
        cg.wait(v, ev_gath)
        v.memset(hin[:], 0.0)
        for r in range(NCORES):
            TS(v, scr[2][:, 0:16], gath[:, 32 * r:32 * r + 16], col(smask, r),
               col(smask, 8 + r), OP.mult, OP.add)
            TS(v, scr[2][:, 16:32], gath[:, 32 * r + 16:32 * r + 32], col(smask, r))
            v.tensor_tensor(out=scr[2][:, 32:48], in0=hin[:, 0:16], in1=scr[2][:, 0:16],
                            op=OP.mult)
            v.tensor_tensor(out=hin[:, 0:16], in0=scr[2][:, 32:48], in1=scr[2][:, 16:32],
                            op=OP.add)
        cv.sig(v.tensor_copy(out=junk[:, 3:4], in_=zeros[:, 0:1]))
    em("dve", f)
    dbg_tap("hin", hin[:, 0:16], gate=cv)

    for c in range(NCH):
        def f(v, c=c):
            v.tensor_tensor_scan(out=scr[2][:, 0:TPC], data0=fbuf[:, TPC * c:TPC * (c + 1)],
                                 data1=ibuf[:, TPC * c:TPC * (c + 1)], initial=col(hin, c),
                                 op0=OP.mult, op1=OP.add)
            vsp(v)
            cv.sig(v.tensor_copy(out=fbuf[:, TPC * c:TPC * (c + 1)], in_=scr[2][:, 0:TPC]))
        em("dve", f)

        def f(s, c=c):
            cv.wait(s)
            s.activation(out=scr[3][:, 0:TPC], in_=fbuf[:, TPC * c:TPC * (c + 1)], func=AF.Silu)
            ca.sig(s.copy(out=fbuf[:, TPC * c:TPC * (c + 1)], in_=scr[3][:, 0:TPC]))
        em("act", f)
    ev_silu = ca.n
    dbg_tap("hch0", fbuf[:, 0:TPC], gate=ca)

    emit_stageA("g", with_ss=False)

    def evac_g(oc, ps):
        def f(v, oc=oc, ps=ps):
            cp.wait(v)
            ca.wait(v)
            cv.sig(v.tensor_tensor(out=scr[3][:, 0:TPC], in0=ps[:, 0:TPC],
                                   in1=scr[1][:, 0:TPC], op=OP.mult))
        em("dve", f)

        def f(s, oc=oc):
            cv.wait(s)
            ca.sig(s.activation(out=scr[2][:, 0:TPC], in_=scr[3][:, 0:TPC], func=AF.Square))
        em("act", f)

        def f(t, oc=oc):
            ca.wait(t)
            t.matmul(out=psS[0:1, 0:512], lhsT=onesf[:, 0:1], rhs=scr[2][:, 0:512],
                     start=(oc == 0), stop=(oc == NCH - 1))
            cp.sig(t.matmul(out=psTf[0:1, 0:512], lhsT=onesf[:, 0:1], rhs=scr[2][:, 512:1024],
                            start=(oc == 0), stop=(oc == NCH - 1)))
        em("pe", f)

        def f(v, oc=oc):
            cp.wait(v)
            ca.wait(v, ev_silu)
            v.scalar_tensor_tensor(out=scr[2][:, 0:TPC], in0=scr[3][:, 0:TPC],
                                   scalar=col(gwcols, oc), in1=fbuf[:, TPC * oc:TPC * (oc + 1)],
                                   op0=OP.mult, op1=OP.mult)
            cv.sig(v.tensor_copy(out=fbuf[:, TPC * oc:TPC * (oc + 1)], in_=scr[2][:, 0:TPC]))
        em("dve", f)
    emit_fm_matmul("g", evac_g)
    dbg_tap("opre0", fbuf[:, 0:TPC], gate=cv)

    def f(s):
        cp.wait(s)
        s.copy(out=gssr[0:1, 0:512], in_=psS[0:1, 0:512])
        ca.sig(s.copy(out=gssr[0:1, 512:1024], in_=psTf[0:1, 0:512]))
    em("act", f)
    ev_gss = ca.n
    dbg_tap("gss", gssr[:], gate=ca)

    def f(sy):
        ca.wait(sy)
        cv.wait(sy)
        clw.dma(sy.dma_start(out=cst[:], in_=nrm_d["o"][:]))
    em("sync", f)

    for t in range(NT):
        for q in range(4):
            def f(te, t=t, q=q):
                cv.wait(te)
                ca.wait(te)
                for j in range(4):
                    c = 4 * q + j
                    ins = te.transpose(out=psTf[:, 128 * j:128 * (j + 1)],
                                       in_=fbuf[:, TPC * c + 128 * t:TPC * c + 128 * (t + 1)],
                                       identity=idf[:])
                cp.sig(ins)
            em("pe", f)

            def f(s, q=q):
                cp.wait(s)
                ca.sig(s.copy(out=scr[0][:, 512 * q:512 * (q + 1)], in_=psTf[:, 0:512]))
            em("act", f)

        def f(te, t=t):
            ca.wait(te)
            cp.sig(te.matmul(out=psTf[:, 0:1], lhsT=gssr[0:1, 128 * t:128 * (t + 1)],
                             rhs=onesf[0:1, 0:1], start=True, stop=True))
        em("pe", f)

        def f(s, t=t):
            cp.wait(s)
            cv.wait(s)
            s.copy(out=col(stats, ST_T0 + 6), in_=psTf[:, 0:1])
            ssp(s)
            s.activation(out=col(stats, ST_T0 + 7), in_=col(stats, ST_T0 + 6),
                         func=AF.Ln, scale=1.0 / float(D), bias=col(biasc, 1))
            ssp(s)
            s.activation(out=col(stats, ST_GRS + t), in_=col(stats, ST_T0 + 7),
                         func=AF.Exp, scale=-0.5)
            s.activation(out=scr[2][:], in_=scr[0][:], func=AF.Square,
                         accum_out=col(stats, ST_T0 + 8))
            ssp(s)
            ca.sig(s.copy(out=junk[:, 2:3], in_=zeros[:, 0:1]))
        em("act", f)

        def f(v, t=t):
            ca.wait(v)
            TS(v, col(stats, ST_T0 + 9), col(stats, ST_T0 + 8), col(stats, ST_GRS + t),
               col(stats, ST_GRS + t), OP.mult, OP.mult)
            cv.sig(v.tensor_copy(out=junk[:, 4:5], in_=zeros[:, 0:1]))
        em("dve", f)

        def f(s, t=t):
            cv.wait(s)
            s.activation(out=col(stats, ST_T0 + 10), in_=col(stats, ST_T0 + 9),
                         func=AF.Ln, scale=1.0 / float(D), bias=col(biasc, 0))
            ssp(s)
            ca.sig(s.activation(out=col(stats, ST_T0 + 11), in_=col(stats, ST_T0 + 10),
                                func=AF.Exp, scale=-0.5))
        em("act", f)

        def f(v, t=t):
            ca.wait(v)
            clw.wait(v)
            v.tensor_tensor(out=scr[1][:], in0=scr[0][:], in1=cst[:], op=OP.mult)
            v.tensor_reduce(out=col(stats, ST_T0), in_=scr[1][:], axis=AX,
                            op=OP.max, apply_absolute_value=True)
            vsp(v)
            TS(v, col(stats, ST_T0 + 1), col(stats, ST_GRS + t), col(stats, ST_T0 + 11))
            vsp(v)
            TS(v, col(stats, ST_T0 + 2), col(stats, ST_T0), col(stats, ST_T0 + 1),
               1e-5, OP.mult, OP.max)
            vsp(v)
            v.reciprocal(out=col(stats, ST_T0 + 3), in_=col(stats, ST_T0 + 2))
            vsp(v)
            TS(v, col(stats, ST_T0 + 4), col(stats, ST_T0 + 3), col(stats, ST_T0 + 1),
               127.0, OP.mult, OP.mult)
            TS(v, col(stats, 104 + t), col(stats, ST_T0 + 2), col(wsc, 4 + 3),
               1.0 / 127.0, OP.mult, OP.mult)
            TS(v, scr[2][:], scr[1][:], col(stats, ST_T0 + 4))
            cv.sig(TS(v, scb[0][:], scr[2][:], MAGIC, MAGIC, OP.add, OP.subtract))
        em("dve", f)
        if t == 0:
            dbg_tap("ko0", scb[0][:], gate=cv)

        def f(te):
            cv.wait(te)
        em("pe", f)
        emit_transpose16_bf(scb[0][:],
                            lambda c, t=t: kbuf[:, TPC * c + 128 * t:TPC * c + 128 * (t + 1)])

    for osp in range(4):
        def f(sy, osp=osp):
            cp.wait(sy)
            for rr in range(2):
                r = 2 * osp + rr
                src = ag1_out[D * r:D * (r + 1), RS * 3:RS * 4]
                cls.dma(sy.dma_start(
                    out=wslab[:].rearrange("p (k j) -> p k j", j=512)[:, :, 256 * rr:256 * (rr + 1)],
                    in_=src.rearrange("(k p) j -> p k j", p=128)))
        em("sync", f)
        for t in range(NT):
            def f(sy, osp=osp, t=t):
                cv.wait(sy)
                clx.dma(sy.dma_start(out=scr[1][:, 0:512],
                                     in_=x_d[128 * t:128 * (t + 1), 512 * osp:512 * (osp + 1)]))
            em("sync", f)

            def f(te, osp=osp, t=t):
                cls.wait(te)
                clx.wait(te)
                cv.wait(te)
                ca.wait(te)
                ps = psA if t % 2 == 0 else psB
                for k in range(NCH):
                    ins = te.matmul(out=ps[:, 0:512],
                                    lhsT=kbuf[:, TPC * k + 128 * t:TPC * k + 128 * (t + 1)],
                                    rhs=wslab[:, 512 * k:512 * (k + 1)],
                                    start=(k == 0), stop=(k == NCH - 1))
                cp.sig(ins)
            em("pe", f)

            def f(v, osp=osp, t=t):
                cp.wait(v)
                ps = psA if t % 2 == 0 else psB
                cv.sig(v.scalar_tensor_tensor(
                    out=fbuf[:, D * t + 512 * osp:D * t + 512 * (osp + 1)],
                    in0=ps[:, 0:512], scalar=col(stats, 104 + t),
                    in1=scr[1][:, 0:512], op0=OP.mult, op1=OP.add))
            em("dve", f)
    dbg_tap("y1t0", fbuf[:, 0:D], gate=cv)

    for t in range(NT):
        def f(v, t=t):
            vsp(v)
            y1 = fbuf[:, D * t:D * (t + 1)]
            v.tensor_reduce(out=col(stats, ST_T0), in_=y1, axis=AX, op=OP.add)
            vsp(v)
            TS(v, col(stats, ST_MU + t), col(stats, ST_T0), 1.0 / float(D))
            vsp(v)
            cv.sig(TS(v, scr[0][:], y1, col(stats, ST_MU + t), None, OP.subtract))
        em("dve", f)

        def f(s, t=t):
            cv.wait(s)
            s.activation(out=scr[2][:], in_=scr[0][:], func=AF.Square,
                         accum_out=col(stats, ST_T0 + 1))
            ssp(s)
            s.activation(out=col(stats, ST_T0 + 2), in_=col(stats, ST_T0 + 1),
                         func=AF.Ln, scale=1.0 / float(D), bias=col(biasc, 1))
            ssp(s)
            ca.sig(s.activation(out=col(stats, ST_RSN + t), in_=col(stats, ST_T0 + 2),
                                func=AF.Exp, scale=-0.5))
        em("act", f)

        def f(v, t=t):
            ca.wait(v)
            TS(v, fbuf[:, D * t:D * (t + 1)], scr[0][:], col(stats, ST_RSN + t))
            cv.sig(TS(v, scb[0][:], scr[0][:], col(stats, ST_RSN + t)))
        em("dve", f)
        if t == 0:
            dbg_tap("xln0", fbuf[:, 0:D], gate=cv)
        if t == 1:
            dbg_tap("xlnt1", fbuf[:, D:2 * D], gate=cv)

        def f(te):
            cv.wait(te)
        em("pe", f)
        emit_transpose16_bf(scb[0][:],
                            lambda c, t=t: kbuf[:, TPC * c + 128 * t:TPC * c + 128 * (t + 1)])

    for grp in range(4):
        for hc in range(NCH):
            if hc % 2 == 0:
                def f(sy, grp=grp, hc=hc):
                    cp.wait(sy)
                    cc.wait(sy, ev_ag2)
                    half = (hc // 2) % 2
                    r = 2 * grp + hc // 8
                    hl = 256 * ((hc // 2) % 4)
                    src = ag2a_out[D * r:D * (r + 1), hl:hl + 256]
                    cls.dma(sy.dma_start(
                        out=wslab[:, 4096 * half:4096 * half + 4096].rearrange(
                            "p (k j) -> p k j", j=256),
                        in_=src.rearrange("(k p) j -> p k j", p=128)))
                em("sync", f)
            half = (hc // 2) % 2
            ps = psA if hc % 2 == 0 else psB
            def f(te, hc=hc, half=half, ps=ps):
                cls.wait(te)
                cv.wait(te)
                ca.wait(te)
                for cpos in range(2):
                    for k in range(NCH):
                        ins = te.matmul(
                            out=ps[:, 512 * cpos:512 * (cpos + 1)],
                            lhsT=wslab[:, 4096 * half + 256 * k + 128 * (hc % 2):
                                       4096 * half + 256 * k + 128 * (hc % 2) + 128],
                            rhs=kbuf[:, TPC * k + 512 * cpos:TPC * k + 512 * (cpos + 1)],
                            start=(k == 0), stop=(k == NCH - 1))
                cp.sig(ins)
            em("pe", f)

            def f(s, ps=ps):
                cp.wait(s)
                cv.wait(s)
                ca.sig(s.activation(out=scr[2][:, 0:TPC], in_=ps[:, 0:TPC], func=AF.Relu))
            em("act", f)

            def f(v, hc=hc):
                ca.wait(v)
                cv.sig(v.tensor_tensor(out=ibuf[:, TPC * hc:TPC * (hc + 1)],
                                       in0=scr[2][:, 0:TPC], in1=scr[2][:, 0:TPC], op=OP.mult))
            em("dve", f)
        if grp == 0:
            dbg_tap("hh0", ibuf[:, 0:TPC], gate=cv)
        if grp == 3:
            dbg_tap("hh31", ibuf[:, TPC:2 * TPC], gate=cv)
        dbg_tap("ibg%d" % grp, ibuf[:], gate=cv)

        for osp in range(4):
            def f(sy, grp=grp, osp=osp):
                cp.wait(sy)
                for rr in range(2):
                    r = 2 * osp + rr
                    src = ag2b_out[H * r + D * grp:H * r + D * (grp + 1), 0:256]
                    cls.dma(sy.dma_start(
                        out=wslab[:].rearrange("p (k j) -> p k j", j=512)[:, :, 256 * rr:256 * (rr + 1)],
                        in_=src.rearrange("(k p) j -> p k j", p=128)))
            em("sync", f)
            for t in range(NT):
                def f(te, osp=osp, t=t):
                    cls.wait(te)
                    cv.wait(te)
                    ca.wait(te)
                    ps = psA if t % 2 == 0 else psB
                    for k in range(NCH):
                        ins = te.matmul(out=ps[:, 0:512],
                                        lhsT=ibuf[:, TPC * k + 128 * t:TPC * k + 128 * (t + 1)],
                                        rhs=wslab[:, 512 * k:512 * (k + 1)],
                                        start=(k == 0), stop=(k == NCH - 1))
                    cp.sig(ins)
                em("pe", f)

                def f(v, osp=osp, t=t):
                    cp.wait(v)
                    ps = psA if t % 2 == 0 else psB
                    dst = fbuf[:, D * t + 512 * osp:D * t + 512 * (osp + 1)]
                    cv.sig(v.tensor_tensor(out=dst, in0=dst, in1=ps[:, 0:512], op=OP.add))
                em("dve", f)

    dbg_tap("v2t1", fbuf[:, D:2 * D], gate=cv)

    for t in range(NT):
        def f(v, t=t):
            vsp(v)
            y2 = fbuf[:, D * t:D * (t + 1)]
            v.tensor_reduce(out=col(stats, ST_T0), in_=y2, axis=AX, op=OP.add)
            vsp(v)
            TS(v, col(stats, ST_MU2 + t), col(stats, ST_T0), 1.0 / float(D))
            vsp(v)
            cv.sig(TS(v, scr[0][:], y2, col(stats, ST_MU2 + t), None, OP.subtract))
        em("dve", f)

        def f(s, t=t):
            cv.wait(s)
            s.activation(out=scr[2][:], in_=scr[0][:], func=AF.Square,
                         accum_out=col(stats, ST_T0 + 1))
            ssp(s)
            s.activation(out=col(stats, ST_T0 + 2), in_=col(stats, ST_T0 + 1),
                         func=AF.Ln, scale=1.0 / float(D), bias=col(biasc, 1))
            ssp(s)
            ca.sig(s.activation(out=col(stats, ST_RSN2 + t), in_=col(stats, ST_T0 + 2),
                                func=AF.Exp, scale=-0.5))
        em("act", f)

        def f(v, t=t):
            ca.wait(v)
            cso.wait(v)
            cv.sig(TS(v, scr[1][:], scr[0][:], col(stats, ST_RSN2 + t)))
        em("dve", f)

        def f(sy, t=t):
            cv.wait(sy)
            cso.dma(sy.dma_start(out=y_d[128 * t:128 * (t + 1), :], in_=scr[1][:]))
        em("sync", f)

    def f(sy):
        cso.wait(sy)
    em("sync", f)

    block.__exit__(None, None, None)
    return nc


_prog_cache = {}


def _get_prog(dbg_names=()):
    key = tuple(dbg_names)
    if key not in _prog_cache:
        _prog_cache[key] = build_program(key)
    return _prog_cache[key]


def make_in_maps(inputs):
    x = np.ascontiguousarray(np.asarray(inputs["L0_attn_hidden_states"],
                                        dtype=np.float32).reshape(NCORES * TPC, D))
    W = {p: np.asarray(inputs["L0_attn_W" + p], np.float32) for p in ("i", "f", "g", "o")}
    wfc = np.asarray(inputs["L0_ffn_w_fc"], np.float32)
    wpj = np.asarray(inputs["L0_ffn_w_proj"], np.float32)
    nrm = {p: np.ascontiguousarray(np.broadcast_to(
        np.asarray(inputs["L0_attn_norm_" + p], np.float32), (128, D)))
        for p in ("i", "f", "g", "o")}
    gw = np.asarray(inputs["L0_attn_g_norm_w"], np.float32)
    gwcols = np.ascontiguousarray(gw.reshape(NCH, 128).T)
    in_maps = []
    for c in range(NCORES):
        sm = np.zeros((128, 16), np.float32)
        pos = c % 4
        base = 4 * (c // 4)
        for r in range(8):
            m = 1.0 if (base <= r < base + pos) else 0.0
            sm[:, r] = m
            sm[:, 8 + r] = 1.0 - m
        im = {
            "x": np.ascontiguousarray(x[TPC * c:TPC * (c + 1)]),
            "wfc": np.ascontiguousarray(wfc[FCS * c:FCS * (c + 1)]),
            "wpj": np.ascontiguousarray(wpj[RS * c:RS * (c + 1)]),
            "gwcols": gwcols,
            "smask": sm,
        }
        for p in ("i", "f", "g", "o"):
            im["w" + p] = np.ascontiguousarray(W[p][RS * c:RS * (c + 1)])
            im["nrm" + p] = nrm[p]
        in_maps.append(im)
    return in_maps


def kernel(**inputs):
    nc = _get_prog(_DBG)
    in_maps = make_in_maps(inputs)
    res = run_bass_kernel_spmd(nc, in_maps, core_ids=list(range(NCORES)))
    y = np.concatenate([r["y"] for r in res.results], axis=0)
    out = np.ascontiguousarray(y.reshape(2, 4096, D).astype(np.float32))
    if _DBG:
        kernel.dbg = [{k: v for k, v in r.items() if k.startswith("dbg_")}
                      for r in res.results]
    return out


# version 09
# speedup vs baseline: 1.1195x; 1.0022x over previous
import numpy as np
import ml_dtypes
import concourse.bass as bass
import concourse.mybir as mybir
from concourse.bass_utils import run_bass_kernel_spmd
from concourse.replica_groups import maybe_share_collective_output_space

F32 = mybir.dt.float32
BF16 = mybir.dt.bfloat16
AF = mybir.ActivationFunctionType
OP = mybir.AluOpType
AX = mybir.AxisListType.X
MAGIC = float(np.float32(1.5 * 2**23))

NCORES = 8
D = 2048
H = 8192
TPC = 1024
NT = 8
NCH = 16
RS = 256
FCS = 1024
SHRINKS = [float(np.float32(s)) for s in np.linspace(0.4, 1.0, 16).astype(np.float32)]
NSH = 16
QB = 64
NB = 32

_DBG = ()
_DBG_SHAPES = {
    "wsc": ([128, 16], F32), "kb0": ([128, 2048], BF16), "cb0": ([128, 64], F32),
    "fch0": ([128, TPC], F32), "iech0": ([128, TPC], BF16), "seg": ([128, 32], F32),
    "hin": ([128, 16], F32), "hch0": ([128, TPC], F32), "gss": ([1, TPC], F32),
    "opre0": ([128, TPC], F32), "y1t0": ([128, D], F32), "xln0": ([128, D], F32),
    "xlnt1": ([128, D], F32), "v2t1": ([128, D], F32), "hh31": ([128, TPC], BF16),
    "ibg0": ([128, 16 * TPC], BF16), "ibg1": ([128, 16 * TPC], BF16),
    "ibg2": ([128, 16 * TPC], BF16), "ibg3": ([128, 16 * TPC], BF16),
    "hh0": ([128, TPC], BF16), "u0": ([128, D], F32), "err0": ([128, NSH * NB], F32),
    "ko0": ([128, D], BF16), "stw": ([128, 4], F32), "row8": ([1, 8], F32),
}


class Ctr:
    def __init__(self, nc, name):
        self.h = nc.semaphore(name).__enter__()
        self.n = 0

    def sig(self, instr, k=1):
        instr.then_inc(self.h, k)
        self.n += k
        return self.n

    def dma(self, instr):
        return self.sig(instr, 16)

    def wait(self, eng, val=None):
        v = self.n if val is None else val
        assert v <= self.n
        if v > 0:
            eng.wait_ge(self.h, v)


def r3(ap, k):
    return ap.rearrange("p (b k) -> p b k", k=k)


def build_program(dbg_names=()):
    nc = bass.Bass(target_bir_lowering=False, detect_race_conditions=False)

    x_d = nc.dram_tensor("x", [TPC, D], F32, kind="ExternalInput").ap()
    w_d = {p: nc.dram_tensor("w" + p, [RS, D], F32, kind="ExternalInput").ap()
           for p in ("i", "f", "g", "o")}
    wfc_d = nc.dram_tensor("wfc", [FCS, D], F32, kind="ExternalInput").ap()
    wpj_d = nc.dram_tensor("wpj", [RS, H], F32, kind="ExternalInput").ap()
    nrm_d = {p: nc.dram_tensor("nrm" + p, [128, D], F32, kind="ExternalInput").ap()
             for p in ("i", "f", "g", "o")}
    gwc_d = nc.dram_tensor("gwcols", [128, NCH], F32, kind="ExternalInput").ap()
    smask_d = nc.dram_tensor("smask", [128, 16], F32, kind="ExternalInput").ap()
    y_d = nc.dram_tensor("y", [TPC, D], F32, kind="ExternalOutput").ap()

    ident_f = nc.inline_tensor(np.eye(128, dtype=np.float32), name="ident_f").ap()
    ident_b = nc.inline_tensor(np.eye(128, dtype=ml_dtypes.bfloat16), name="ident_b").ap()
    ones_f = nc.inline_tensor(np.ones((128, 128), np.float32), name="ones_f").ap()
    bias_np = np.zeros((128, 2), np.float32); bias_np[:, 0] = 1e-8; bias_np[:, 1] = 1e-5
    bias_d = nc.inline_tensor(bias_np, name="bias_c").ap()

    RG = [list(range(NCORES))]
    sp_ar = maybe_share_collective_output_space("AllReduce", RG)
    sp_ag = maybe_share_collective_output_space("AllGather", RG)
    ag0_in = nc.dram_tensor("ag0_in", [1, 8], F32, kind="Internal").ap()
    ag0_out = nc.dram_tensor("ag0_out", [1, 8], F32, kind="Internal", addr_space=sp_ar).ap()
    ag1_in = nc.dram_tensor("ag1_in", [D, 4 * RS], BF16, kind="Internal").ap()
    ag1_out = nc.dram_tensor("ag1_out", [NCORES * D, 4 * RS], BF16, kind="Internal", addr_space=sp_ag).ap()
    ag2a_in = nc.dram_tensor("ag2a_in", [D, FCS], BF16, kind="Internal").ap()
    ag2a_out = nc.dram_tensor("ag2a_out", [NCORES * D, FCS], BF16, kind="Internal", addr_space=sp_ag).ap()
    ag2b_in = nc.dram_tensor("ag2b_in", [H, RS], BF16, kind="Internal").ap()
    ag2b_out = nc.dram_tensor("ag2b_out", [NCORES * H, RS], BF16, kind="Internal", addr_space=sp_ag).ap()
    ag3_in = nc.dram_tensor("ag3_in", [128, 32], F32, kind="Internal").ap()
    ag3_out = nc.dram_tensor("ag3_out", [NCORES * 128, 32], F32, kind="Internal", addr_space=sp_ag).ap()

    dbg = {nm: nc.dram_tensor("dbg_" + nm, _DBG_SHAPES[nm][0], _DBG_SHAPES[nm][1],
                              kind="ExternalOutput").ap() for nm in dbg_names}

    def S(name, shape, dt):
        return nc.sbuf_tensor(name, shape, dt).__enter__()

    kbuf = S("kbuf", [128, NCH * TPC], BF16)
    fbuf = S("fbuf", [128, NCH * TPC], F32)
    ibuf = S("ibuf", [128, NCH * TPC], BF16)
    scr = [S(f"scr{i}", [128, D], F32) for i in range(4)]
    scb = [S("scb0", [128, D], BF16)]
    cst = S("cst", [128, D], F32)
    wslab = S("wslab", [128, 8192], BF16)
    wsv = wslab[:, 0:2048]
    wsv2 = wslab[:, 2048:4096]
    zeros = S("zeros_t", [128, TPC], F32)
    idf = S("idf", [128, 128], F32)
    idb = S("idb", [128, 128], BF16)
    onesf = S("onesf", [128, 128], F32)
    gwcols = S("gwcols_s", [128, NCH], F32)
    smask = S("smask_s", [128, 16], F32)
    stats = S("stats", [128, 128], F32)
    errst = S("errst", [128, NSH * NB], F32)
    eqb = S("eqb", [128, 256], F32)
    eqc = S("eqc", [128, NSH * NB], F32)
    cbs = S("cbs", [128, 64], F32)
    gath = S("gath", [128, 256], F32)
    hin = S("hin", [128, NCH], F32)
    junk = S("junk_t", [128, 8], F32)
    wsc = S("wsc", [128, 16], F32)
    row8 = S("row8", [1, 8], F32)
    biasc = S("biasc", [128, 2], F32)
    gssr = S("gssr", [1, TPC], F32)

    ST_SS = 0
    ST_RSTD = 8
    ST_GRS = 24
    ST_MU = 48
    ST_RSN = 56
    ST_W = 64
    ST_T0 = 68
    ST_MU2 = 80
    ST_RSN2 = 88

    psA = nc.psum_tensor("psA", [128, TPC], F32).__enter__()
    psB = nc.psum_tensor("psB", [128, TPC], F32).__enter__()
    psTb = nc.psum_tensor("psTb", [128, 1024], BF16).__enter__()
    psTf = nc.psum_tensor("psTf", [128, 512], F32).__enter__()
    psS = nc.psum_tensor("psS", [128, 512], F32).__enter__()

    clx = Ctr(nc, "clx")
    clw = Ctr(nc, "clw")
    cls = Ctr(nc, "cls")
    cs0 = Ctr(nc, "cs0")
    cs1 = Ctr(nc, "cs1")
    cs2 = Ctr(nc, "cs2")
    cs3 = Ctr(nc, "cs3")
    cso = Ctr(nc, "cso")
    cg = Ctr(nc, "cg")
    cc = Ctr(nc, "cc")
    cp = Ctr(nc, "cp")
    cv = Ctr(nc, "cv")
    ca = Ctr(nc, "ca")
    cq = Ctr(nc, "cq")
    cpl = Ctr(nc, "cpl")

    block = nc.Block().__enter__()
    ENG = {"sync": "sync", "dve": "vector", "act": "scalar", "pe": "tensor", "pool": "gpsimd"}

    DATAPATH = {"tensor_copy", "tensor_tensor", "tensor_scalar", "tensor_reduce",
                "scalar_tensor_tensor", "tensor_tensor_scan", "reciprocal", "memset",
                "memzero", "activation", "copy", "mul", "add", "square", "sqrt",
                "tensor_add", "tensor_sub", "tensor_mul", "tensor_max", "select",
                "tensor_single_scalar", "tensor_relu", "reduce_max", "reduce_sum"}

    class DrainWrap:
        def __init__(self, eng):
            self._e = eng

        def __getattr__(self, n):
            a = getattr(self._e, n)
            if n in DATAPATH:
                def call(*args, _a=a, **kw):
                    ins = _a(*args, **kw)
                    need = True
                    try:
                        out = kw.get("out")
                        if out is None and args:
                            out = args[0]
                        sh = out.shape
                        free = 1
                        for d in sh[1:]:
                            free *= int(d)
                        need = free < 512 or kw.get("accum_out") is not None
                    except Exception:
                        need = True
                    if need:
                        return self._e.drain()
                    return ins
                return call
            return a

    def em(engine, fn, wrap=True):
        if wrap and engine in ("dve", "act", "pool"):
            getattr(block, ENG[engine])(lambda e: fn(DrainWrap(e)))
        else:
            getattr(block, ENG[engine])(fn)

    def vsp(v):
        v.tensor_copy(out=junk[:, 0:1], in_=zeros[:, 0:1])

    def ssp(s):
        pass

    def TS(v, out, in0, s1, s2=None, op0=OP.mult, op1=None):
        if op1 is None:
            return v.tensor_scalar(out=out, in0=in0, scalar1=s1, scalar2=None, op0=op0)
        return v.tensor_scalar(out=out, in0=in0, scalar1=s1, scalar2=s2, op0=op0, op1=op1)

    def col(t, c):
        return t[:, c:c + 1]

    def dbg_tap(name, src_ap, gate=None):
        if name in dbg:
            def f(sy, name=name, src=src_ap, gate=gate):
                if gate is not None:
                    gate.wait(sy)
                cso.dma(sy.dma_start(out=dbg[name][:], in_=src))
            em("sync", f)
            for e in ("dve", "act", "pe", "pool"):
                def g(eng):
                    cso.wait(eng)
                em(e, g)

    def emit_transpose16_bf(src_tile, dst_fn):
        for half in range(2):
            def f(t, half=half, src=src_tile):
                ca.wait(t)
                for j in range(8):
                    c = 8 * half + j
                    ins = t.transpose(out=psTb[:, 128 * j:128 * (j + 1)],
                                      in_=src[:, 128 * c:128 * (c + 1)], identity=idb[:])
                cp.sig(ins)
            em("pe", f)

            def f(s, half=half, dst_fn=dst_fn):
                cp.wait(s)
                for j in range(8):
                    c = 8 * half + j
                    ins = s.copy(out=dst_fn(c), in_=psTb[:, 128 * j:128 * (j + 1)])
                ca.sig(ins)
            em("act", f)

    def f(sy):
        clw.dma(sy.dma_start(out=idf[:], in_=ident_f[:]))
        clw.dma(sy.dma_start(out=idb[:], in_=ident_b[:]))
        clw.dma(sy.dma_start(out=onesf[:], in_=ones_f[:]))
        clw.dma(sy.dma_start(out=gwcols[:], in_=gwc_d[:]))
        clw.dma(sy.dma_start(out=smask[:], in_=smask_d[:]))
        clw.dma(sy.dma_start(out=biasc[:], in_=bias_d[:]))
    em("sync", f)

    def f(v):
        clw.wait(v)
        v.memset(zeros[:], 0.0)
        v.memset(junk[:], 0.0)
        v.memset(stats[:], 0.0)
        v.memset(eqb[:], 0.0)
        v.memset(errst[:], 0.0)
        v.memset(eqc[:], 0.0)
        v.memset(cbs[:], 0.0)
        v.memset(hin[:], 0.0)
        v.memset(gath[:], 0.0)
        v.memset(wsc[:], 0.0)
        cv.sig(v.memset(gssr[:], 0.0))
    em("dve", f)
    ev_consts = cv.n

    for wi, p in enumerate("ifgo"):
        def f(sy, p=p):
            cv.wait(sy)
            clw.dma(sy.dma_start(out=scr[0][:], in_=w_d[p][0:128, :]))
            clw.dma(sy.dma_start(out=scr[1][:], in_=w_d[p][128:256, :]))
        em("sync", f)

        def f(v, wi=wi):
            clw.wait(v)
            cv.wait(v, ev_consts)
            v.tensor_reduce(out=col(stats, ST_T0), in_=scr[0][:], axis=AX, op=OP.add,
                            apply_absolute_value=True)
            v.tensor_reduce(out=col(stats, ST_T0 + 1), in_=scr[1][:], axis=AX, op=OP.add,
                            apply_absolute_value=True)
            vsp(v)
            cv.sig(v.tensor_tensor(out=col(stats, ST_W + wi), in0=col(stats, ST_T0),
                                   in1=col(stats, ST_T0 + 1), op=OP.add))
        em("dve", f)
    ev_wparts = cv.n
    dbg_tap("stw", stats[:, ST_W:ST_W + 4], gate=cv)

    def f(t):
        cv.wait(t, ev_wparts)
        cp.sig(t.matmul(out=psS[0:1, 0:4], lhsT=onesf[:, 0:1],
                        rhs=stats[:, ST_W:ST_W + 4], start=True, stop=True))
    em("pe", f)
    ev_pp = cp.n

    def f(s):
        cp.wait(s, ev_pp)
        s.memzero(row8[:])
        ca.sig(s.copy(out=row8[0:1, 0:4], in_=psS[0:1, 0:4]))
    em("act", f)
    ev_row8 = ca.n

    def f(sy):
        ca.wait(sy, ev_row8)
        cs0.dma(sy.dma_start(out=ag0_in[:], in_=row8[:]))
    em("sync", f)
    ev_ag0st = cs0.n

    def f(g):
        cs0.wait(g, ev_ag0st)
        cc.sig(g.collective_compute("AllReduce", OP.add, replica_groups=RG,
                                    ins=[ag0_in[:].opt()], outs=[ag0_out[:].opt()]))
        cc.wait(g)
        cg.dma(g.dma_start(out=row8[:], in_=ag0_out[:]))
    em("pool", f)
    ev_ag0 = cg.n

    def f(v):
        cg.wait(v, ev_ag0)
        TS(v, row8[0:1, 4:8], row8[0:1, 0:4], 1.0 / float(D * D), 1e-5, OP.mult, OP.max)
        vsp(v)
        cv.sig(v.reciprocal(out=row8[0:1, 0:4], in_=row8[0:1, 4:8]))
    em("dve", f)
    ev_wsc1 = cv.n
    dbg_tap("row8", row8[:], gate=cv)

    def f(t):
        cv.wait(t, ev_wsc1)
        cp.sig(t.matmul(out=psS[:, 0:8], lhsT=onesf[0:1, :], rhs=row8[0:1, 0:8],
                        start=True, stop=True))
    em("pe", f)
    ev_wsc2 = cp.n

    def f(s):
        cp.wait(s, ev_wsc2)
        ca.sig(s.copy(out=wsc[:, 0:8], in_=psS[:, 0:8]))
    em("act", f)
    ev_wsc = ca.n
    dbg_tap("wsc", wsc[:], gate=ca)

    for wi, p in enumerate("ifgo"):
        for rt in range(2):
            def f(sy, p=p, rt=rt):
                cv.wait(sy)
                ca.wait(sy)
                cs1.wait(sy)
                clw.dma(sy.dma_start(out=scr[0][:], in_=w_d[p][128 * rt:128 * (rt + 1), :]))
            em("sync", f)

            def f(v, wi=wi):
                clw.wait(v)
                ca.wait(v, ev_wsc)
                TS(v, scr[1][:], scr[0][:], col(wsc, wi))
                TS(v, scr[2][:], scr[1][:], MAGIC, MAGIC, OP.add, OP.subtract)
                cv.sig(TS(v, scb[0][:], scr[2][:], 1.0, -1.0, OP.min, OP.max))
            em("dve", f)

            def f(t):
                cv.wait(t)
            em("pe", f)
            emit_transpose16_bf(scb[0][:], lambda c: wsv[:, 128 * c:128 * (c + 1)])

            def f(sy, wi=wi, rt=rt):
                ca.wait(sy)
                c0 = 256 * wi + 128 * rt
                cs1.dma(sy.dma_start(
                    out=ag1_in[:, c0:c0 + 128].rearrange("(c p) j -> p c j", p=128),
                    in_=r3(wsv, 128)))
            em("sync", f)
    ev_t_staged = cs1.n

    def f(g):
        cs1.wait(g, ev_t_staged)
        cc.sig(g.collective_compute("AllGather", OP.bypass, replica_groups=RG,
                                    ins=[ag1_in[:].opt()], outs=[ag1_out[:].opt()]))
    em("pool", f)
    ev_ag1 = cc.n

    ALPH = [float(a) for a in np.float32(np.linspace(0.4, 1.0, 16))]
    C15 = float(np.float32(np.float32(1.0) / np.float32(15.0)))
    for ti in range(16):
        if ti < 8:
            src = wfc_d[128 * ti:128 * (ti + 1), :]
        else:
            rt, csl = (ti - 8) // 4, (ti - 8) % 4
            src = wpj_d[128 * rt:128 * (rt + 1), 2048 * csl:2048 * (csl + 1)]

        def f(sy, src=src):
            cv.wait(sy)
            ca.wait(sy)
            cs1.wait(sy)
            cs2.wait(sy)
            clw.dma(sy.dma_start(out=scr[0][:], in_=src))
        em("sync", f)

        def f(v):
            clw.wait(v)
            v.tensor_reduce(out=eqb[:, 0:32], in_=r3(scr[0][:], QB), axis=AX,
                            op=OP.max, apply_absolute_value=True)
            vsp(v)
            cv.sig(TS(v, eqb[:, 32:64], eqb[:, 0:32], 1e-8, C15,
                      OP.max, OP.mult))
        em("dve", f)

        dT = (fbuf[:, 0:2048], fbuf[:, 2048:4096])
        qT3 = (fbuf[:, 4096:6144], fbuf[:, 6144:8192], fbuf[:, 8192:10240])
        SKC4 = (eqc[:, 0:32], eqc[:, 192:224], eqc[:, 256:288], eqc[:, 384:416])
        RKC2 = (eqc[:, 32:64], eqc[:, 224:256])
        BERR = eqc[:, 64:96]
        BSC = eqc[:, 96:128]
        DTMP2 = (eqc[:, 128:160], eqc[:, 352:384])
        DIF = (eqc[:, 160:192], eqc[:, 320:352])

        def sel_ops(v, m):
            pass

        for k in range(NSH):
            def f(v, k=k):
                j = k - 2
                ej = errst[:, 32 * j:32 * (j + 1)]
                if k == 0:
                    TS(v, SKC4[0], eqb[:, 32:64], ALPH[0])
                    v.drain()
                    v.reciprocal(out=RKC2[0], in_=SKC4[0])
                    v.drain()
                rke = r3(RKC2[k % 2], 1).broadcast_to([128, NB, QB])
                ske = r3(SKC4[k % 4], 1).broadcast_to([128, NB, QB])
                v.tensor_tensor(out=r3(scr[1][:], QB), in0=r3(scr[0][:], QB),
                                in1=rke, op=OP.mult)
                if k < NSH - 1:
                    TS(v, SKC4[(k + 1) % 4], eqb[:, 32:64], ALPH[k + 1])
                if k >= 2:
                    ca.wait(v)
                    v.tensor_reduce(out=ej, in_=r3(qT3[j % 3], QB),
                                    axis=AX, op=OP.add)
                if k >= 4:
                    v.tensor_tensor(out=DIF[(k - 3) % 2], in0=DIF[(k - 3) % 2],
                                    in1=DTMP2[(k - 3) % 2], op=OP.mult)
                TS(v, scr[2][:], scr[1][:], MAGIC, MAGIC, OP.add, OP.subtract)
                if k < NSH - 1:
                    v.reciprocal(out=RKC2[(k + 1) % 2], in_=SKC4[(k + 1) % 4])
                if k >= 4:
                    v.tensor_tensor(out=BSC, in0=BSC, in1=DIF[(k - 3) % 2],
                                    op=OP.add)
                TS(v, scr[2][:], scr[2][:], 15.0, -15.0, OP.min, OP.max)
                if k == 2:
                    v.tensor_copy(out=BERR, in_=errst[:, 0:32])
                if k >= 3:
                    v.tensor_tensor(out=DTMP2[j % 2], in0=ej, in1=BERR,
                                    op=OP.subtract)
                v.tensor_tensor(out=r3(scr[3][:], QB), in0=r3(scr[2][:], QB),
                                in1=ske, op=OP.mult)
                if k == 2:
                    v.tensor_copy(out=BSC, in_=SKC4[0])
                if k >= 3:
                    TS(v, DTMP2[j % 2], DTMP2[j % 2], 0.0, None, OP.is_lt)
                cv.sig(v.tensor_tensor(out=dT[k % 2], in0=scr[3][:],
                                       in1=scr[0][:], op=OP.subtract))
                if k >= 3:
                    v.tensor_tensor(out=DIF[j % 2], in0=SKC4[j % 4],
                                    in1=BSC, op=OP.subtract)
                    v.tensor_tensor(out=BERR, in0=BERR, in1=ej, op=OP.min)
            em("dve", f, wrap=False)

            def f(s, k=k):
                cv.wait(s)
                ca.sig(s.activation(out=qT3[k % 3], in_=dT[k % 2], func=AF.Square))
            em("act", f)

        def f(v):
            def wj():
                v.tensor_copy(out=scr[1][:, 0:512], in_=scr[0][:, 0:512])
            e14 = errst[:, 32 * 14:32 * 15]
            e15 = errst[:, 32 * 15:32 * 16]
            ca.wait(v, ca.n - 1)
            v.tensor_reduce(out=e14, in_=r3(qT3[14 % 3], QB), axis=AX, op=OP.add)
            wj()
            v.tensor_tensor(out=DIF[1], in0=DIF[1], in1=DTMP2[1], op=OP.mult)
            wj()
            v.tensor_tensor(out=BSC, in0=BSC, in1=DIF[1], op=OP.add)
            wj()
            v.tensor_tensor(out=DTMP2[0], in0=e14, in1=BERR, op=OP.subtract)
            v.drain()
            TS(v, DTMP2[0], DTMP2[0], 0.0, None, OP.is_lt)
            wj()
            v.tensor_tensor(out=DIF[0], in0=SKC4[14 % 4], in1=BSC, op=OP.subtract)
            v.drain()
            v.tensor_tensor(out=DIF[0], in0=DIF[0], in1=DTMP2[0], op=OP.mult)
            wj()
            v.tensor_tensor(out=BERR, in0=BERR, in1=e14, op=OP.min)
            ca.wait(v)
            v.tensor_reduce(out=e15, in_=r3(qT3[15 % 3], QB), axis=AX, op=OP.add)
            wj()
            v.tensor_tensor(out=BSC, in0=BSC, in1=DIF[0], op=OP.add)
            wj()
            v.tensor_tensor(out=DTMP2[1], in0=e15, in1=BERR, op=OP.subtract)
            v.drain()
            TS(v, DTMP2[1], DTMP2[1], 0.0, None, OP.is_lt)
            wj()
            v.tensor_tensor(out=DIF[1], in0=SKC4[15 % 4], in1=BSC, op=OP.subtract)
            v.drain()
            v.tensor_tensor(out=DIF[1], in0=DIF[1], in1=DTMP2[1], op=OP.mult)
            wj()
            v.tensor_tensor(out=BSC, in0=BSC, in1=DIF[1], op=OP.add)
            cv.sig(v.drain())
        em("dve", f, wrap=False)
        if ti == 0:
            dbg_tap("err0", errst[:], gate=cv)

        def f(v):
            sb = eqc[:, 96:128]
            vsp(v)
            v.reciprocal(out=eqc[:, 0:32], in_=sb)
            vsp(v)
            v.tensor_copy(out=cbs[:, 32:64], in_=sb)
            rse = r3(eqc[:, 0:32], 1).broadcast_to([128, NB, QB])
            sbe = r3(sb, 1).broadcast_to([128, NB, QB])
            v.tensor_tensor(out=r3(scr[1][:], QB), in0=r3(scr[0][:], QB),
                            in1=rse, op=OP.mult)
            TS(v, scr[2][:], scr[1][:], MAGIC, MAGIC, OP.add, OP.subtract)
            TS(v, scr[2][:], scr[2][:], 15.0, -15.0, OP.min, OP.max)
            v.tensor_tensor(out=r3(scr[3][:], QB), in0=r3(scr[2][:], QB),
                            in1=sbe, op=OP.mult)
            cv.sig(v.tensor_copy(out=wsv2, in_=scr[3][:]))
        em("dve", f)
        if ti == 0:
            dbg_tap("cb0", cbs[:], gate=cv)

        def f(t):
            cv.wait(t)
        em("pe", f)
        emit_transpose16_bf(wsv2, lambda c: wsv[:, 128 * c:128 * (c + 1)])

        def f(sy, ti=ti):
            ca.wait(sy)
            if ti < 8:
                dst = ag2a_in[:, 128 * ti:128 * (ti + 1)]
            else:
                rt, csl = (ti - 8) // 4, (ti - 8) % 4
                dst = ag2b_in[2048 * csl:2048 * (csl + 1), 128 * rt:128 * (rt + 1)]
            cs2.dma(sy.dma_start(out=dst.rearrange("(c p) j -> p c j", p=128),
                                 in_=r3(wsv, 128)))
        em("sync", f)
    ev_ffn_staged = cs2.n

    def f(g):
        cs2.wait(g, ev_ffn_staged)
        cc.sig(g.collective_compute("AllGather", OP.bypass, replica_groups=RG,
                                    ins=[ag2a_in[:].opt()], outs=[ag2a_out[:].opt()]))
        cc.sig(g.collective_compute("AllGather", OP.bypass, replica_groups=RG,
                                    ins=[ag2b_in[:].opt()], outs=[ag2b_out[:].opt()]))
    em("pool", f)
    ev_ag2 = cc.n

    W_IDX = {"i": 0, "f": 1, "g": 2, "o": 3}

    def emit_stageA(p, with_ss):
        def f(sy, p=p):
            cv.wait(sy)
            ca.wait(sy)
            clw.dma(sy.dma_start(out=cst[:], in_=nrm_d[p][:]))
        em("sync", f)
        for t in range(NT):
            def f(sy, t=t):
                cv.wait(sy)
                ca.wait(sy)
                clx.dma(sy.dma_start(out=scr[0][:], in_=x_d[128 * t:128 * (t + 1), :]))
            em("sync", f)
            if with_ss:
                def f(s, t=t):
                    clx.wait(s)
                    cv.wait(s)
                    s.activation(out=scr[2][:], in_=scr[0][:], func=AF.Square,
                                 accum_out=col(stats, ST_SS + t))
                    ssp(s)
                    s.activation(out=col(stats, ST_T0 + 2), in_=col(stats, ST_SS + t),
                                 func=AF.Ln, scale=1.0 / float(D), bias=col(biasc, 0))
                    ssp(s)
                    ca.sig(s.activation(out=col(stats, ST_RSTD + t), in_=col(stats, ST_T0 + 2),
                                        func=AF.Exp, scale=-0.5))
                em("act", f)

            def f(v, t=t, p=p):
                clx.wait(v)
                clw.wait(v)
                ca.wait(v)
                wi = W_IDX[p]
                v.tensor_tensor(out=scr[1][:], in0=scr[0][:], in1=cst[:], op=OP.mult)
                vsp(v)
                v.tensor_reduce(out=col(stats, ST_T0), in_=scr[1][:], axis=AX,
                                op=OP.max, apply_absolute_value=True)
                vsp(v)
                TS(v, col(stats, ST_T0 + 1), col(stats, ST_T0), col(stats, ST_RSTD + t),
                   1e-5, OP.mult, OP.max)
                vsp(v)
                v.reciprocal(out=col(stats, ST_T0 + 3), in_=col(stats, ST_T0 + 1))
                vsp(v)
                TS(v, col(stats, ST_T0 + 4), col(stats, ST_T0 + 3), col(stats, ST_RSTD + t),
                   127.0, OP.mult, OP.mult)
                TS(v, col(stats, 96 + t), col(stats, ST_T0 + 1), col(wsc, 4 + wi),
                   1.0 / 127.0, OP.mult, OP.mult)
                TS(v, scr[2][:], scr[1][:], col(stats, ST_T0 + 4))
                cv.sig(TS(v, scb[0][:], scr[2][:], MAGIC, MAGIC, OP.add, OP.subtract))
            em("dve", f)

            def f(te):
                cv.wait(te)
            em("pe", f)
            emit_transpose16_bf(scb[0][:],
                                lambda c, t=t: kbuf[:, TPC * c + 128 * t:TPC * c + 128 * (t + 1)])

    def emit_cexp():
        crow = scr[1][0:1, TPC:TPC + TPC]
        for t in range(NT):
            def f(te, t=t):
                cv.wait(te)
                ca.wait(te)
                cp.sig(te.matmul(out=psTf[0:1, 0:128], lhsT=col(stats, 96 + t),
                                 rhs=idf[:], start=True, stop=True))
            em("pe", f)

            def f(sz, t=t, crow=crow):
                cp.wait(sz)
                ca.sig(sz.copy(out=crow[0:1, 128 * t:128 * (t + 1)], in_=psTf[0:1, 0:128]))
            em("act", f)

        def f(te, crow=crow):
            ca.wait(te)
            te.matmul(out=psB[:, 0:512], lhsT=onesf[0:1, :], rhs=crow[0:1, 0:512],
                      start=True, stop=True)
            cp.sig(te.matmul(out=psB[:, 512:1024], lhsT=onesf[0:1, :],
                             rhs=crow[0:1, 512:1024], start=True, stop=True))
        em("pe", f)

        def f(sz):
            cp.wait(sz)
            ca.sig(sz.copy(out=scr[1][:, 0:TPC], in_=psB[:, 0:TPC]))
        em("act", f)

    def emit_fm_matmul(p, evac):
        wi = W_IDX[p]
        emit_cexp()
        for oc in range(NCH):
            if oc % 2 == 0:
                r = oc // 2
                half = r % 2
                def f(sy, r=r, half=half, wi=wi):
                    cp.wait(sy)
                    cs1.wait(sy)
                    cs2.wait(sy)
                    cc.wait(sy, ev_ag1)
                    src = ag1_out[D * r:D * (r + 1), RS * wi:RS * (wi + 1)]
                    cls.dma(sy.dma_start(
                        out=wslab[:, 4096 * half:4096 * half + 4096].rearrange(
                            "p (k j) -> p k j", j=256),
                        in_=src.rearrange("(k p) j -> p k j", p=128)))
                em("sync", f)
            half = (oc // 2) % 2
            ps = psA if oc % 2 == 0 else psB
            def f(t, oc=oc, half=half, ps=ps):
                cls.wait(t)
                cv.wait(t)
                ca.wait(t)
                for cpos in range(2):
                    for k in range(NCH):
                        ins = t.matmul(
                            out=ps[:, 512 * cpos:512 * (cpos + 1)],
                            lhsT=wslab[:, 4096 * half + 256 * k + 128 * (oc % 2):
                                       4096 * half + 256 * k + 128 * (oc % 2) + 128],
                            rhs=kbuf[:, TPC * k + 512 * cpos:TPC * k + 512 * (cpos + 1)],
                            start=(k == 0), stop=(k == NCH - 1))
                cp.sig(ins)
            em("pe", f)
            evac(oc, ps)

    emit_stageA("f", with_ss=True)

    def evac_f(oc, ps):
        def f(v, oc=oc, ps=ps):
            cp.wait(v)
            ca.wait(v)
            cv.sig(v.tensor_tensor(out=scr[2][:, 0:TPC], in0=ps[:, 0:TPC],
                                   in1=scr[1][:, 0:TPC], op=OP.mult))
        em("dve", f)

        def f(s, oc=oc):
            cv.wait(s)
            ca.sig(s.activation(out=fbuf[:, TPC * oc:TPC * (oc + 1)],
                                in_=scr[2][:, 0:TPC], func=AF.Sigmoid))
        em("act", f)
    emit_fm_matmul("f", evac_f)
    dbg_tap("fch0", fbuf[:, 0:TPC], gate=ca)

    emit_stageA("i", with_ss=False)

    def evac_i(oc, ps):
        def f(v, oc=oc, ps=ps):
            cp.wait(v)
            ca.wait(v)
            cv.sig(v.tensor_tensor(out=scr[3][:, 0:TPC], in0=ps[:, 0:TPC],
                                   in1=scr[1][:, 0:TPC], op=OP.mult))
        em("dve", f)

        def f(s, oc=oc):
            cv.wait(s)
            ca.sig(s.activation(out=scr[2][:, 0:TPC], in_=scr[3][:, 0:TPC], func=AF.Silu))
        em("act", f)

        def f(v, oc=oc):
            ca.wait(v)
            TS(v, scr[3][:, 0:TPC], fbuf[:, TPC * oc:TPC * (oc + 1)], -1.0, 1.0,
               OP.mult, OP.add)
            cv.sig(v.tensor_tensor(out=ibuf[:, TPC * oc:TPC * (oc + 1)],
                                   in0=scr[2][:, 0:TPC], in1=scr[3][:, 0:TPC], op=OP.mult))
        em("dve", f)
    emit_fm_matmul("i", evac_i)
    dbg_tap("iech0", ibuf[:, 0:TPC], gate=cv)

    for c in range(NCH):
        def f(v, c=c):
            if c == 0:
                ca.wait(v)
            v.tensor_tensor_scan(out=scr[2][:, 0:TPC], data0=fbuf[:, TPC * c:TPC * (c + 1)],
                                 data1=zeros[:, 0:TPC], initial=1.0, op0=OP.mult, op1=OP.add)
            v.tensor_tensor_scan(out=scr[3][:, 0:TPC], data0=fbuf[:, TPC * c:TPC * (c + 1)],
                                 data1=ibuf[:, TPC * c:TPC * (c + 1)], initial=0.0,
                                 op0=OP.mult, op1=OP.add)
            v.tensor_copy(out=col(cbs, c), in_=scr[2][:, TPC - 1:TPC])
            ins = v.tensor_copy(out=col(cbs, 16 + c), in_=scr[3][:, TPC - 1:TPC])
            if c == NCH - 1:
                cv.sig(ins)
        em("dve", f)
    dbg_tap("seg", cbs[:, 0:32], gate=cv)

    def f(sy):
        cv.wait(sy)
        cs3.dma(sy.dma_start(out=ag3_in[:], in_=cbs[:, 0:32]))
    em("sync", f)
    ev_ag3st = cs3.n

    def f(g):
        cs3.wait(g, ev_ag3st)
        cc.sig(g.collective_compute("AllGather", OP.bypass, replica_groups=RG,
                                    ins=[ag3_in[:].opt()], outs=[ag3_out[:].opt()]))
        cc.wait(g)
        cg.dma(g.dma_start(out=gath[:].rearrange("p (g j) -> p g j", j=32),
                           in_=ag3_out[:].rearrange("(g p) j -> p g j", p=128)))
    em("pool", f)
    ev_gath = cg.n

    def f(v):
        cg.wait(v, ev_gath)
        v.memset(hin[:], 0.0)
        for r in range(NCORES):
            TS(v, scr[2][:, 0:16], gath[:, 32 * r:32 * r + 16], col(smask, r),
               col(smask, 8 + r), OP.mult, OP.add)
            TS(v, scr[2][:, 16:32], gath[:, 32 * r + 16:32 * r + 32], col(smask, r))
            v.tensor_tensor(out=scr[2][:, 32:48], in0=hin[:, 0:16], in1=scr[2][:, 0:16],
                            op=OP.mult)
            v.tensor_tensor(out=hin[:, 0:16], in0=scr[2][:, 32:48], in1=scr[2][:, 16:32],
                            op=OP.add)
        cv.sig(v.tensor_copy(out=junk[:, 3:4], in_=zeros[:, 0:1]))
    em("dve", f)
    dbg_tap("hin", hin[:, 0:16], gate=cv)

    for c in range(NCH):
        def f(v, c=c):
            v.tensor_tensor_scan(out=scr[2][:, 0:TPC], data0=fbuf[:, TPC * c:TPC * (c + 1)],
                                 data1=ibuf[:, TPC * c:TPC * (c + 1)], initial=col(hin, c),
                                 op0=OP.mult, op1=OP.add)
            vsp(v)
            cv.sig(v.tensor_copy(out=fbuf[:, TPC * c:TPC * (c + 1)], in_=scr[2][:, 0:TPC]))
        em("dve", f)

        def f(s, c=c):
            cv.wait(s)
            s.activation(out=scr[3][:, 0:TPC], in_=fbuf[:, TPC * c:TPC * (c + 1)], func=AF.Silu)
            ca.sig(s.copy(out=fbuf[:, TPC * c:TPC * (c + 1)], in_=scr[3][:, 0:TPC]))
        em("act", f)
    ev_silu = ca.n
    dbg_tap("hch0", fbuf[:, 0:TPC], gate=ca)

    emit_stageA("g", with_ss=False)

    def evac_g(oc, ps):
        def f(v, oc=oc, ps=ps):
            cp.wait(v)
            ca.wait(v)
            cv.sig(v.tensor_tensor(out=scr[3][:, 0:TPC], in0=ps[:, 0:TPC],
                                   in1=scr[1][:, 0:TPC], op=OP.mult))
        em("dve", f)

        def f(s, oc=oc):
            cv.wait(s)
            ca.sig(s.activation(out=scr[2][:, 0:TPC], in_=scr[3][:, 0:TPC], func=AF.Square))
        em("act", f)

        def f(t, oc=oc):
            ca.wait(t)
            t.matmul(out=psS[0:1, 0:512], lhsT=onesf[:, 0:1], rhs=scr[2][:, 0:512],
                     start=(oc == 0), stop=(oc == NCH - 1))
            cp.sig(t.matmul(out=psTf[0:1, 0:512], lhsT=onesf[:, 0:1], rhs=scr[2][:, 512:1024],
                            start=(oc == 0), stop=(oc == NCH - 1)))
        em("pe", f)

        def f(v, oc=oc):
            cp.wait(v)
            ca.wait(v, ev_silu)
            v.scalar_tensor_tensor(out=scr[2][:, 0:TPC], in0=scr[3][:, 0:TPC],
                                   scalar=col(gwcols, oc), in1=fbuf[:, TPC * oc:TPC * (oc + 1)],
                                   op0=OP.mult, op1=OP.mult)
            cv.sig(v.tensor_copy(out=fbuf[:, TPC * oc:TPC * (oc + 1)], in_=scr[2][:, 0:TPC]))
        em("dve", f)
    emit_fm_matmul("g", evac_g)
    dbg_tap("opre0", fbuf[:, 0:TPC], gate=cv)

    def f(s):
        cp.wait(s)
        s.copy(out=gssr[0:1, 0:512], in_=psS[0:1, 0:512])
        ca.sig(s.copy(out=gssr[0:1, 512:1024], in_=psTf[0:1, 0:512]))
    em("act", f)
    ev_gss = ca.n
    dbg_tap("gss", gssr[:], gate=ca)

    def f(sy):
        ca.wait(sy)
        cv.wait(sy)
        clw.dma(sy.dma_start(out=cst[:], in_=nrm_d["o"][:]))
    em("sync", f)

    for t in range(NT):
        for q in range(4):
            def f(te, t=t, q=q):
                cv.wait(te)
                ca.wait(te)
                for j in range(4):
                    c = 4 * q + j
                    ins = te.transpose(out=psTf[:, 128 * j:128 * (j + 1)],
                                       in_=fbuf[:, TPC * c + 128 * t:TPC * c + 128 * (t + 1)],
                                       identity=idf[:])
                cp.sig(ins)
            em("pe", f)

            def f(s, q=q):
                cp.wait(s)
                ca.sig(s.copy(out=scr[0][:, 512 * q:512 * (q + 1)], in_=psTf[:, 0:512]))
            em("act", f)

        def f(te, t=t):
            ca.wait(te)
            cp.sig(te.matmul(out=psTf[:, 0:1], lhsT=gssr[0:1, 128 * t:128 * (t + 1)],
                             rhs=onesf[0:1, 0:1], start=True, stop=True))
        em("pe", f)

        def f(s, t=t):
            cp.wait(s)
            cv.wait(s)
            s.copy(out=col(stats, ST_T0 + 6), in_=psTf[:, 0:1])
            ssp(s)
            s.activation(out=col(stats, ST_T0 + 7), in_=col(stats, ST_T0 + 6),
                         func=AF.Ln, scale=1.0 / float(D), bias=col(biasc, 1))
            ssp(s)
            s.activation(out=col(stats, ST_GRS + t), in_=col(stats, ST_T0 + 7),
                         func=AF.Exp, scale=-0.5)
            s.activation(out=scr[2][:], in_=scr[0][:], func=AF.Square,
                         accum_out=col(stats, ST_T0 + 8))
            ssp(s)
            ca.sig(s.copy(out=junk[:, 2:3], in_=zeros[:, 0:1]))
        em("act", f)

        def f(v, t=t):
            ca.wait(v)
            TS(v, col(stats, ST_T0 + 9), col(stats, ST_T0 + 8), col(stats, ST_GRS + t),
               col(stats, ST_GRS + t), OP.mult, OP.mult)
            cv.sig(v.tensor_copy(out=junk[:, 4:5], in_=zeros[:, 0:1]))
        em("dve", f)

        def f(s, t=t):
            cv.wait(s)
            s.activation(out=col(stats, ST_T0 + 10), in_=col(stats, ST_T0 + 9),
                         func=AF.Ln, scale=1.0 / float(D), bias=col(biasc, 0))
            ssp(s)
            ca.sig(s.activation(out=col(stats, ST_T0 + 11), in_=col(stats, ST_T0 + 10),
                                func=AF.Exp, scale=-0.5))
        em("act", f)

        def f(v, t=t):
            ca.wait(v)
            clw.wait(v)
            v.tensor_tensor(out=scr[1][:], in0=scr[0][:], in1=cst[:], op=OP.mult)
            v.tensor_reduce(out=col(stats, ST_T0), in_=scr[1][:], axis=AX,
                            op=OP.max, apply_absolute_value=True)
            vsp(v)
            TS(v, col(stats, ST_T0 + 1), col(stats, ST_GRS + t), col(stats, ST_T0 + 11))
            vsp(v)
            TS(v, col(stats, ST_T0 + 2), col(stats, ST_T0), col(stats, ST_T0 + 1),
               1e-5, OP.mult, OP.max)
            vsp(v)
            v.reciprocal(out=col(stats, ST_T0 + 3), in_=col(stats, ST_T0 + 2))
            vsp(v)
            TS(v, col(stats, ST_T0 + 4), col(stats, ST_T0 + 3), col(stats, ST_T0 + 1),
               127.0, OP.mult, OP.mult)
            TS(v, col(stats, 104 + t), col(stats, ST_T0 + 2), col(wsc, 4 + 3),
               1.0 / 127.0, OP.mult, OP.mult)
            TS(v, scr[2][:], scr[1][:], col(stats, ST_T0 + 4))
            cv.sig(TS(v, scb[0][:], scr[2][:], MAGIC, MAGIC, OP.add, OP.subtract))
        em("dve", f)
        if t == 0:
            dbg_tap("ko0", scb[0][:], gate=cv)

        def f(te):
            cv.wait(te)
        em("pe", f)
        emit_transpose16_bf(scb[0][:],
                            lambda c, t=t: kbuf[:, TPC * c + 128 * t:TPC * c + 128 * (t + 1)])

    for osp in range(4):
        def f(sy, osp=osp):
            cp.wait(sy)
            for rr in range(2):
                r = 2 * osp + rr
                src = ag1_out[D * r:D * (r + 1), RS * 3:RS * 4]
                cls.dma(sy.dma_start(
                    out=wslab[:].rearrange("p (k j) -> p k j", j=512)[:, :, 256 * rr:256 * (rr + 1)],
                    in_=src.rearrange("(k p) j -> p k j", p=128)))
        em("sync", f)
        for t in range(NT):
            def f(sy, osp=osp, t=t):
                cv.wait(sy)
                clx.dma(sy.dma_start(out=scr[1][:, 0:512],
                                     in_=x_d[128 * t:128 * (t + 1), 512 * osp:512 * (osp + 1)]))
            em("sync", f)

            def f(te, osp=osp, t=t):
                cls.wait(te)
                clx.wait(te)
                cv.wait(te)
                ca.wait(te)
                ps = psA if t % 2 == 0 else psB
                for k in range(NCH):
                    ins = te.matmul(out=ps[:, 0:512],
                                    lhsT=kbuf[:, TPC * k + 128 * t:TPC * k + 128 * (t + 1)],
                                    rhs=wslab[:, 512 * k:512 * (k + 1)],
                                    start=(k == 0), stop=(k == NCH - 1))
                cp.sig(ins)
            em("pe", f)

            def f(v, osp=osp, t=t):
                cp.wait(v)
                ps = psA if t % 2 == 0 else psB
                cv.sig(v.scalar_tensor_tensor(
                    out=fbuf[:, D * t + 512 * osp:D * t + 512 * (osp + 1)],
                    in0=ps[:, 0:512], scalar=col(stats, 104 + t),
                    in1=scr[1][:, 0:512], op0=OP.mult, op1=OP.add))
            em("dve", f)
    dbg_tap("y1t0", fbuf[:, 0:D], gate=cv)

    for t in range(NT):
        def f(v, t=t):
            vsp(v)
            y1 = fbuf[:, D * t:D * (t + 1)]
            v.tensor_reduce(out=col(stats, ST_T0), in_=y1, axis=AX, op=OP.add)
            vsp(v)
            TS(v, col(stats, ST_MU + t), col(stats, ST_T0), 1.0 / float(D))
            vsp(v)
            cv.sig(TS(v, scr[0][:], y1, col(stats, ST_MU + t), None, OP.subtract))
        em("dve", f)

        def f(s, t=t):
            cv.wait(s)
            s.activation(out=scr[2][:], in_=scr[0][:], func=AF.Square,
                         accum_out=col(stats, ST_T0 + 1))
            ssp(s)
            s.activation(out=col(stats, ST_T0 + 2), in_=col(stats, ST_T0 + 1),
                         func=AF.Ln, scale=1.0 / float(D), bias=col(biasc, 1))
            ssp(s)
            ca.sig(s.activation(out=col(stats, ST_RSN + t), in_=col(stats, ST_T0 + 2),
                                func=AF.Exp, scale=-0.5))
        em("act", f)

        def f(v, t=t):
            ca.wait(v)
            TS(v, fbuf[:, D * t:D * (t + 1)], scr[0][:], col(stats, ST_RSN + t))
            cv.sig(TS(v, scb[0][:], scr[0][:], col(stats, ST_RSN + t)))
        em("dve", f)
        if t == 0:
            dbg_tap("xln0", fbuf[:, 0:D], gate=cv)
        if t == 1:
            dbg_tap("xlnt1", fbuf[:, D:2 * D], gate=cv)

        def f(te):
            cv.wait(te)
        em("pe", f)
        emit_transpose16_bf(scb[0][:],
                            lambda c, t=t: kbuf[:, TPC * c + 128 * t:TPC * c + 128 * (t + 1)])

    for grp in range(4):
        for hc in range(NCH):
            if hc % 2 == 0:
                def f(sy, grp=grp, hc=hc):
                    cp.wait(sy)
                    cc.wait(sy, ev_ag2)
                    half = (hc // 2) % 2
                    r = 2 * grp + hc // 8
                    hl = 256 * ((hc // 2) % 4)
                    src = ag2a_out[D * r:D * (r + 1), hl:hl + 256]
                    cls.dma(sy.dma_start(
                        out=wslab[:, 4096 * half:4096 * half + 4096].rearrange(
                            "p (k j) -> p k j", j=256),
                        in_=src.rearrange("(k p) j -> p k j", p=128)))
                em("sync", f)
            half = (hc // 2) % 2
            ps = psA if hc % 2 == 0 else psB
            def f(te, hc=hc, half=half, ps=ps):
                cls.wait(te)
                cv.wait(te)
                ca.wait(te)
                for cpos in range(2):
                    for k in range(NCH):
                        ins = te.matmul(
                            out=ps[:, 512 * cpos:512 * (cpos + 1)],
                            lhsT=wslab[:, 4096 * half + 256 * k + 128 * (hc % 2):
                                       4096 * half + 256 * k + 128 * (hc % 2) + 128],
                            rhs=kbuf[:, TPC * k + 512 * cpos:TPC * k + 512 * (cpos + 1)],
                            start=(k == 0), stop=(k == NCH - 1))
                cp.sig(ins)
            em("pe", f)

            def f(s, ps=ps):
                cp.wait(s)
                cv.wait(s)
                ca.sig(s.activation(out=scr[2][:, 0:TPC], in_=ps[:, 0:TPC], func=AF.Relu))
            em("act", f)

            def f(v, hc=hc):
                ca.wait(v)
                cv.sig(v.tensor_tensor(out=ibuf[:, TPC * hc:TPC * (hc + 1)],
                                       in0=scr[2][:, 0:TPC], in1=scr[2][:, 0:TPC], op=OP.mult))
            em("dve", f)
        if grp == 0:
            dbg_tap("hh0", ibuf[:, 0:TPC], gate=cv)
        if grp == 3:
            dbg_tap("hh31", ibuf[:, TPC:2 * TPC], gate=cv)
        dbg_tap("ibg%d" % grp, ibuf[:], gate=cv)

        for osp in range(4):
            def f(sy, grp=grp, osp=osp):
                cp.wait(sy)
                for rr in range(2):
                    r = 2 * osp + rr
                    src = ag2b_out[H * r + D * grp:H * r + D * (grp + 1), 0:256]
                    cls.dma(sy.dma_start(
                        out=wslab[:].rearrange("p (k j) -> p k j", j=512)[:, :, 256 * rr:256 * (rr + 1)],
                        in_=src.rearrange("(k p) j -> p k j", p=128)))
            em("sync", f)
            for t in range(NT):
                def f(te, osp=osp, t=t):
                    cls.wait(te)
                    cv.wait(te)
                    ca.wait(te)
                    ps = psA if t % 2 == 0 else psB
                    for k in range(NCH):
                        ins = te.matmul(out=ps[:, 0:512],
                                        lhsT=ibuf[:, TPC * k + 128 * t:TPC * k + 128 * (t + 1)],
                                        rhs=wslab[:, 512 * k:512 * (k + 1)],
                                        start=(k == 0), stop=(k == NCH - 1))
                    cp.sig(ins)
                em("pe", f)

                def f(v, osp=osp, t=t):
                    cp.wait(v)
                    ps = psA if t % 2 == 0 else psB
                    dst = fbuf[:, D * t + 512 * osp:D * t + 512 * (osp + 1)]
                    cv.sig(v.tensor_tensor(out=dst, in0=dst, in1=ps[:, 0:512], op=OP.add))
                em("dve", f)

    dbg_tap("v2t1", fbuf[:, D:2 * D], gate=cv)

    _ln2cnt = []
    for t in range(NT):
        obuf = scr[1] if t % 2 == 0 else scr[3]

        def f(v, t=t):
            vsp(v)
            y2 = fbuf[:, D * t:D * (t + 1)]
            v.tensor_reduce(out=col(stats, ST_T0), in_=y2, axis=AX, op=OP.add)
            vsp(v)
            TS(v, col(stats, ST_MU2 + t), col(stats, ST_T0), 1.0 / float(D))
            vsp(v)
            cv.sig(TS(v, scr[0][:], y2, col(stats, ST_MU2 + t), None, OP.subtract))
        em("dve", f)

        def f(s, t=t):
            cv.wait(s)
            s.activation(out=scr[2][:], in_=scr[0][:], func=AF.Square,
                         accum_out=col(stats, ST_T0 + 1))
            ssp(s)
            s.activation(out=col(stats, ST_T0 + 2), in_=col(stats, ST_T0 + 1),
                         func=AF.Ln, scale=1.0 / float(D), bias=col(biasc, 1))
            ssp(s)
            ca.sig(s.activation(out=col(stats, ST_RSN2 + t), in_=col(stats, ST_T0 + 2),
                                func=AF.Exp, scale=-0.5))
        em("act", f)

        def f(v, t=t, obuf=obuf):
            ca.wait(v)
            if t >= 2:
                cso.wait(v, _ln2cnt[t - 2])
            cv.sig(TS(v, obuf[:], scr[0][:], col(stats, ST_RSN2 + t)))
        em("dve", f)

        def f(sy, t=t, obuf=obuf):
            cv.wait(sy)
            cso.dma(sy.dma_start(out=y_d[128 * t:128 * (t + 1), :], in_=obuf[:]))
        em("sync", f)
        _ln2cnt.append(cso.n)

    def f(sy):
        cso.wait(sy)
    em("sync", f)

    block.__exit__(None, None, None)
    return nc


_prog_cache = {}


def _get_prog(dbg_names=()):
    key = tuple(dbg_names)
    if key not in _prog_cache:
        _prog_cache[key] = build_program(key)
    return _prog_cache[key]


def make_in_maps(inputs):
    x = np.ascontiguousarray(np.asarray(inputs["L0_attn_hidden_states"],
                                        dtype=np.float32).reshape(NCORES * TPC, D))
    W = {p: np.asarray(inputs["L0_attn_W" + p], np.float32) for p in ("i", "f", "g", "o")}
    wfc = np.asarray(inputs["L0_ffn_w_fc"], np.float32)
    wpj = np.asarray(inputs["L0_ffn_w_proj"], np.float32)
    nrm = {p: np.ascontiguousarray(np.broadcast_to(
        np.asarray(inputs["L0_attn_norm_" + p], np.float32), (128, D)))
        for p in ("i", "f", "g", "o")}
    gw = np.asarray(inputs["L0_attn_g_norm_w"], np.float32)
    gwcols = np.ascontiguousarray(gw.reshape(NCH, 128).T)
    in_maps = []
    for c in range(NCORES):
        sm = np.zeros((128, 16), np.float32)
        pos = c % 4
        base = 4 * (c // 4)
        for r in range(8):
            m = 1.0 if (base <= r < base + pos) else 0.0
            sm[:, r] = m
            sm[:, 8 + r] = 1.0 - m
        im = {
            "x": np.ascontiguousarray(x[TPC * c:TPC * (c + 1)]),
            "wfc": np.ascontiguousarray(wfc[FCS * c:FCS * (c + 1)]),
            "wpj": np.ascontiguousarray(wpj[RS * c:RS * (c + 1)]),
            "gwcols": gwcols,
            "smask": sm,
        }
        for p in ("i", "f", "g", "o"):
            im["w" + p] = np.ascontiguousarray(W[p][RS * c:RS * (c + 1)])
            im["nrm" + p] = nrm[p]
        in_maps.append(im)
    return in_maps


def kernel(**inputs):
    nc = _get_prog(_DBG)
    in_maps = make_in_maps(inputs)
    res = run_bass_kernel_spmd(nc, in_maps, core_ids=list(range(NCORES)))
    y = np.concatenate([r["y"] for r in res.results], axis=0)
    out = np.ascontiguousarray(y.reshape(2, 4096, D).astype(np.float32))
    if _DBG:
        kernel.dbg = [{k: v for k, v in r.items() if k.startswith("dbg_")}
                      for r in res.results]
    return out
